# version 02
# baseline (speedup 1.0000x reference)
import numpy as np
import concourse.bass as bass
import concourse.mybir as mybir
from concourse.bass_utils import run_bass_kernel_spmd

F32 = mybir.dt.float32
BF16 = mybir.dt.bfloat16
I32 = mybir.dt.int32
AF = mybir.ActivationFunctionType
ALU = mybir.AluOpType
AX = mybir.AxisListType

S_LEN = 4096
D = 1024
I_FF = 2816
NKC = D // 128
NIC = I_FF // 128
T1 = 512
NT1 = S_LEN // T1
NB1 = T1 // 128
T2 = 128
NT2 = S_LEN // T2

ENGINES = ("sync", "gpsimd", "vector", "scalar", "tensor")


class Ev:
    __slots__ = ("eng", "emit", "deps", "hard", "dma", "needs_inc", "inc_val")

    def __init__(self, eng, emit, deps, hard, dma):
        self.eng = eng
        self.emit = emit
        self.deps = deps
        self.hard = hard
        self.dma = dma
        self.needs_inc = dma
        self.inc_val = None


def _flat(deps):
    out = []
    for d in deps:
        if d is None:
            continue
        if isinstance(d, (list, tuple)):
            out.extend(_flat(d))
        else:
            out.append(d)
    return out


class Prog:
    def __init__(self):
        self.events = {e: [] for e in ENGINES}

    def op(self, eng, emit, deps=(), dma=False, hard=()):
        ev = Ev(eng, emit, _flat(deps), _flat(hard), dma)
        self.events[eng].append(ev)
        return ev

    def finalize(self, block, sems):
        for eng in ENGINES:
            for ev in self.events[eng]:
                for d in ev.deps:
                    if d.eng != ev.eng or d.dma:
                        d.needs_inc = True
                for d in ev.hard:
                    d.needs_inc = True
        for eng in ENGINES:
            c, dc = 0, 0
            for ev in self.events[eng]:
                if ev.dma:
                    dc += 1
                    ev.inc_val = 16 * dc
                elif ev.needs_inc:
                    c += 1
                    ev.inc_val = c

        def emit_stream(eng_obj, eng_name):
            hw = {}
            for ev in self.events[eng_name]:
                for d in ev.deps + ev.hard:
                    if d not in ev.hard and d.eng == ev.eng and (
                            not d.dma or ev.dma):
                        continue
                    sname = ("d_" if d.dma else "c_") + d.eng
                    v = d.inc_val
                    if hw.get(sname, -1) >= v:
                        continue
                    eng_obj.wait_ge(sems[sname], v)
                    hw[sname] = v
                inst = ev.emit(eng_obj)
                if ev.needs_inc:
                    assert inst is not None, f"{eng_name} event needs inc"
                    sname = ("d_" if ev.dma else "c_") + ev.eng
                    inst.then_inc(sems[sname], 16 if ev.dma else 1)

        for name in ENGINES:
            getattr(block, name)(lambda eng, n=name: emit_stream(eng, n))


def emit_quant(p, deps, x_ap, F, sg, qa, qb, qc, bmax, ssc, rcp, r_ap, out_ap,
               hard0=()):
    nb = F // 16
    b3 = x_ap.rearrange("p (s n) -> p s n", n=16)
    e_bm = p.op("vector", lambda v: v.tensor_reduce(
        bmax[:, :nb], b3, axis=AX.X, op=ALU.max, apply_absolute_value=True),
        deps=deps, hard=hard0)
    e_s = p.op("vector", lambda v: v.tensor_scalar(
        ssc[:, :nb], bmax[:, :nb], 1.0 / 6.0, 1e-12, ALU.mult, ALU.max),
        hard=[e_bm])
    e_rc = p.op("vector", lambda v: v.reciprocal(rcp[:, :nb], ssc[:, :nb]),
                hard=[e_s])
    rb = rcp[:, :nb].unsqueeze(-1).broadcast_to((128, nb, 16))
    e_r = p.op("vector", lambda v: v.scalar_tensor_tensor(
        r_ap.rearrange("p (s n) -> p s n", n=16), x_ap, 4.0, rb,
        ALU.mult, ALU.mult), deps=deps, hard=[e_rc])
    e_sg = p.op("scalar", lambda s: s.activation(sg[:, :F], r_ap, AF.Sign),
                deps=[e_r])
    e_u = p.op("scalar", lambda s: s.activation(r_ap, r_ap, AF.Abs),
               deps=[e_sg])
    u = r_ap
    e_t = p.op("vector", lambda v: v.tensor_scalar(
        qa[:, :F], u, 1.0, None, ALU.is_gt), deps=[e_u])
    for m, dst, src in ((3.0, qb, qa), (5.0, qa, qb), (7.0, qb, qa)):
        e_t = p.op("vector", lambda v, m=m, dst=dst, src=src:
                   v.scalar_tensor_tensor(dst[:, :F], u, m, src[:, :F],
                                          ALU.is_gt, ALU.add), deps=[e_t])
    e_last = e_t
    for m, d in ((10.0, 2.0), (14.0, 2.0), (20.0, 4.0)):
        e_v = p.op("vector", lambda v, m=m, d=d: v.tensor_scalar(
            qc[:, :F], u, m, d, ALU.is_gt, ALU.mult), deps=[e_last])
        e_last = p.op("vector", lambda v: v.tensor_tensor(
            qb[:, :F], qb[:, :F], qc[:, :F], ALU.add), deps=[e_v])
    e_vlast = e_last
    e_z = p.op("vector", lambda v: v.tensor_tensor(
        qb[:, :F], qb[:, :F], sg[:, :F], ALU.mult), deps=[e_last, e_sg])
    sb = ssc[:, :nb].unsqueeze(-1).broadcast_to((128, nb, 16))
    e_q = p.op("vector", lambda v: v.scalar_tensor_tensor(
        out_ap.rearrange("p (s n) -> p s n", n=16),
        qb[:, :F].rearrange("p (s n) -> p s n", n=16), 0.5, sb,
        ALU.mult, ALU.mult), deps=[e_z])
    return {"q": e_q, "r": e_r, "u": e_u, "v_last": e_vlast, "sg": e_sg,
            "z": e_z}


def build_program(dbg=False):
    scratch_kind = "ExternalOutput" if dbg else "Internal"
    nc = bass.Bass("TRN2", target_bir_lowering=False, debug=False)
    x_d = nc.dram_tensor("x", [S_LEN, D], F32, kind="ExternalInput").ap()
    wqkv_d = nc.dram_tensor("wqkv", [D, 3 * D], F32, kind="ExternalInput").ap()
    wout_d = nc.dram_tensor("wout", [D, D], F32, kind="ExternalInput").ap()
    bfull_d = nc.dram_tensor("bfull", [128, D], F32, kind="ExternalInput").ap()
    wg_d = nc.dram_tensor("wg", [I_FF, D], F32, kind="ExternalInput").ap()
    wu_d = nc.dram_tensor("wu", [I_FF, D], F32, kind="ExternalInput").ap()
    wd_d = nc.dram_tensor("wd", [D, I_FF], F32, kind="ExternalInput").ap()
    out_d = nc.dram_tensor("out", [S_LEN, D], F32, kind="ExternalOutput").ap()
    x2s_d = nc.dram_tensor("x2s", [S_LEN, D], F32, kind=scratch_kind).ap()
    wgq_d = nc.dram_tensor("wgq_s", [128, NKC * I_FF], BF16, kind=scratch_kind).ap()
    wuq_d = nc.dram_tensor("wuq_s", [128, NKC * I_FF], BF16, kind=scratch_kind).ap()
    wdq_d = nc.dram_tensor("wdq_s", [128, NIC * D], BF16, kind=scratch_kind).ap()

    p = Prog()
    dbg_tokens = {}

    def dump(name, ap, deps, shape, dt):
        if not dbg:
            return None
        dd = nc.dram_tensor("dbg_" + name, list(shape), dt,
                            kind="ExternalOutput").ap()
        ev = p.op("sync", lambda sy: sy.dma_start(out=dd[:, :], in_=ap),
                  deps=deps, dma=True)
        dbg_tokens[name] = ev
        return ev

    from contextlib import ExitStack

    with ExitStack() as st:
        sems = {}
        for e in ENGINES:
            sems["c_" + e] = st.enter_context(nc.semaphore("c_" + e))
            sems["d_" + e] = st.enter_context(nc.semaphore("d_" + e))

        iota_t = st.enter_context(nc.sbuf_tensor("iota_t", [128, 128], I32))
        ident = st.enter_context(nc.sbuf_tensor("ident", [128, 128], BF16))
        mask = st.enter_context(nc.sbuf_tensor("mask", [128, 128], F32))
        e_io = p.op("gpsimd", lambda g: g.iota(
            iota_t[:], pattern=[[1, 128]], base=0, channel_multiplier=-1))
        e_id = p.op("vector", lambda v: v.tensor_single_scalar(
            ident[:], iota_t[:], 0.0, ALU.is_equal), deps=[e_io])
        e_mk = p.op("vector", lambda v: v.tensor_scalar(
            mask[:], iota_t[:], 0.0, -3.0e38, ALU.is_gt, ALU.mult),
            deps=[e_io])

        block = st.enter_context(nc.Block())

        with ExitStack() as st0:
            wch = [st0.enter_context(nc.sbuf_tensor(f"wch{i}", [128, I_FF], F32))
                   for i in range(2)]
            r_s = st0.enter_context(nc.sbuf_tensor("r0", [128, I_FF], F32))
            sg_s = st0.enter_context(nc.sbuf_tensor("sg0", [128, I_FF], BF16))
            qa = st0.enter_context(nc.sbuf_tensor("qa0", [128, I_FF], BF16))
            qb = st0.enter_context(nc.sbuf_tensor("qb0", [128, I_FF], BF16))
            qc = st0.enter_context(nc.sbuf_tensor("qc0", [128, I_FF], BF16))
            bmax = st0.enter_context(nc.sbuf_tensor("bmax0", [128, I_FF // 16], F32))
            ssc = st0.enter_context(nc.sbuf_tensor("ssc0", [128, I_FF // 16], F32))
            rcp = st0.enter_context(nc.sbuf_tensor("rcp0", [128, I_FF // 16], F32))
            wqb = st0.enter_context(nc.sbuf_tensor("wqb0", [128, I_FF], BF16))
            strips = st0.enter_context(nc.sbuf_tensor(
                "strips0", [128, NKC * I_FF], BF16))
            pst = st0.enter_context(nc.psum_tensor("pst0", [128, 1024], BF16))

            st_prev = None

            def do_weight(w_dram, n_chunks, F, n_tp, scratch_d, st_prev_dep):
                ld_reuse = [None, None]
                q_prev = None
                tp_prev = st_prev_dep
                last_ev = None
                for c in range(n_chunks):
                    buf = wch[c % 2]
                    e_ld = p.op("sync", lambda sy, b=buf, c=c: sy.dma_start(
                        out=b[:, :F], in_=w_dram[c * 128:(c + 1) * 128, :]),
                        deps=[ld_reuse[c % 2]], dma=True)
                    qt = emit_quant(p, [e_ld, q_prev], buf[:, :F], F,
                                    sg_s, qa, qb, qc, bmax, ssc, rcp,
                                    r_s[:, :F], wqb[:, :F])
                    ld_reuse[c % 2] = qt["r"]
                    ev_list = []
                    for b0 in range(0, n_tp, 8):
                        bn = min(8, n_tp - b0)
                        tps = []
                        for j in range(bn):
                            e_tp = p.op("tensor", lambda t, j=j, b0=b0: t.transpose(
                                pst[:, j * 128:(j + 1) * 128],
                                wqb[:, (b0 + j) * 128:(b0 + j + 1) * 128],
                                ident[:]), deps=[qt["q"], e_id, tp_prev])
                            tps.append(e_tp)
                        e_ev = p.op("scalar", lambda s, b0=b0, bn=bn, c=c,
                                    n_chunks=n_chunks: s.copy(
                            strips[:].rearrange("p (t f) -> p t f",
                                                f=n_chunks * 128)
                            [:, b0:b0 + bn, c * 128:(c + 1) * 128],
                            pst[:, :bn * 128].rearrange("p (t f) -> p t f",
                                                        f=128)), deps=tps)
                        tp_prev = e_ev
                        ev_list.append(e_ev)
                        q_prev = tps[-1]
                    last_ev = ev_list[-1]
                e_st = p.op("sync", lambda sy: sy.dma_start(
                    out=scratch_d[:, :n_tp * n_chunks * 128],
                    in_=strips[:, :n_tp * n_chunks * 128]),
                    deps=[last_ev], dma=True)
                return e_st, q_prev, last_ev

            e_g, _, _ = do_weight(wg_d, NIC, D, NKC, wgq_d, None)
            e_u_, _, _ = do_weight(wu_d, NIC, D, NKC, wuq_d, e_g)
            e_dw, p0_lastT, p0_lastS = do_weight(wd_d, NKC, I_FF, NIC, wdq_d,
                                                 e_u_)
            p0_dma = [e_g, e_u_, e_dw]

        with ExitStack() as st1:
            wqkv_sb = st1.enter_context(
                nc.sbuf_tensor("wqkv_sb", [128, NKC * 3 * D], BF16))
            woutT = st1.enter_context(nc.sbuf_tensor("woutT", [128, NKC * D], BF16))
            bfull = st1.enter_context(nc.sbuf_tensor("bfull_sb", [128, D], F32))
            with ExitStack() as st1a:
                woutb = st1a.enter_context(
                    nc.sbuf_tensor("woutb", [128, NKC * D], BF16))
                pstw = st1a.enter_context(nc.psum_tensor("pstw", [128, 1024], BF16))
                e_wq_ld = []
                for kc in range(NKC):
                    e1 = p.op("gpsimd", lambda g, kc=kc: g.dma_start(
                        out=wqkv_sb[:, kc * 3 * D:(kc + 1) * 3 * D],
                        in_=wqkv_d[kc * 128:(kc + 1) * 128, :]),
                        deps=p0_dma, dma=True)
                    e2 = p.op("gpsimd", lambda g, kc=kc: g.dma_start(
                        out=woutb[:, kc * D:(kc + 1) * D],
                        in_=wout_d[kc * 128:(kc + 1) * 128, :]),
                        deps=p0_dma, dma=True)
                    e_wq_ld += [e1, e2]
                e_bf_ld = p.op("sync", lambda sy: sy.dma_start(
                    out=bfull[:], in_=bfull_d[:, :]), deps=p0_dma, dma=True)
                prev_ev = p0_lastS
                e_wt_ev = []
                for jc in range(NKC):
                    tps = []
                    for ic in range(NKC):
                        e_tp = p.op("tensor", lambda t, ic=ic, jc=jc: t.transpose(
                            pstw[:, ic * 128:(ic + 1) * 128],
                            woutb[:, ic * D + jc * 128: ic * D + jc * 128 + 128],
                            ident[:]),
                            deps=e_wq_ld + [e_id, prev_ev, p0_lastT]
                            if ic == 0 and jc == 0 else
                            (e_wq_ld + [prev_ev] if ic == 0 else []))
                        tps.append(e_tp)
                    e_ev = p.op("scalar", lambda s, jc=jc: s.copy(
                        woutT[:, jc * D:(jc + 1) * D], pstw[:]), deps=tps)
                    prev_ev = e_ev
                    e_wt_ev.append(e_ev)

            xf = [st1.enter_context(nc.sbuf_tensor(f"xf{i}", [128, NB1 * D], F32))
                  for i in range(2)]
            xbf = st1.enter_context(nc.sbuf_tensor("xbf", [128, NB1 * D], BF16))
            xT = st1.enter_context(nc.sbuf_tensor("xT", [128, NKC * T1], BF16))
            q_sb = st1.enter_context(nc.sbuf_tensor("q_sb", [128, NKC * T1], BF16))
            k_sb = st1.enter_context(nc.sbuf_tensor("k_sb", [128, NKC * T1], BF16))
            v_sb = st1.enter_context(nc.sbuf_tensor("v_sb", [128, NB1 * D], BF16))
            ao_sb = st1.enter_context(nc.sbuf_tensor("ao_sb", [128, NKC * T1], BF16))
            xb_sb = st1.enter_context(nc.sbuf_tensor("xb_sb", [128, NB1 * D], F32))
            y_sb = st1.enter_context(nc.sbuf_tensor("y_sb", [128, NB1 * D], F32))
            sm_sb = st1.enter_context(nc.sbuf_tensor("sm_sb", [128, 128], F32))
            pn_sb = st1.enter_context(nc.sbuf_tensor("pn_sb", [128, 128], BF16))
            pT_sb = st1.enter_context(nc.sbuf_tensor("pT_sb", [128, 128], BF16))
            nmax = st1.enter_context(nc.sbuf_tensor("nmax", [128, 1], F32))
            rsum = st1.enter_context(nc.sbuf_tensor("rsum", [128, 1], F32))
            rred = st1.enter_context(nc.sbuf_tensor("rred", [128, NB1], F32))
            negm = st1.enter_context(nc.sbuf_tensor("negm", [128, NB1], F32))
            ssq = st1.enter_context(nc.sbuf_tensor("ssq", [128, NB1], F32))
            rstd = st1.enter_context(nc.sbuf_tensor("rstd", [128, NB1], F32))
            psA = [st1.enter_context(nc.psum_tensor(f"psA{i}", [128, 512], F32))
                   for i in range(4)]
            psT = st1.enter_context(nc.psum_tensor("psT1", [128, 1024], BF16))

            wq_dep = e_wq_ld + [e_bf_ld, e_mk]
            tl = {"xf_free": [None, None], "y_store": None, "xbf_free": None,
                  "psA_ev": [None] * 4, "psT_ev": e_wt_ev[-1], "ln_s": None}
            p1_lastV = p1_lastS = p1_lastT = None
            p1_dma = None

            for tt in range(NT1):
                pb = tt % 2
                e_lds = []
                for b in range(NB1):
                    e = p.op("sync", lambda sy, b=b, tt=tt, pb=pb: sy.dma_start(
                        out=xf[pb][:, b * D:(b + 1) * D],
                        in_=x_d[tt * T1 + b * 128: tt * T1 + (b + 1) * 128, :]),
                        deps=[tl["xf_free"][pb]] +
                             ([e_wt_ev[-1]] if tt < 2 else []), dma=True)
                    e_lds.append(e)
                e_cast = p.op("scalar", lambda s, pb=pb: s.copy(
                    xbf[:], xf[pb][:]), deps=e_lds + [tl["xbf_free"]])
                e_xb = p.op("vector", lambda v, pb=pb: v.tensor_tensor(
                    xb_sb[:].rearrange("p (b f) -> p b f", f=D),
                    xf[pb][:].rearrange("p (b f) -> p b f", f=D),
                    bfull[:].unsqueeze(1).broadcast_to((128, NB1, D)), ALU.add),
                    deps=e_lds + [e_bf_ld, tl["ln_s"]])

                e_xT_ev = []
                dep_psT = tl["psT_ev"]
                for b in range(NB1):
                    tps = []
                    for kc in range(NKC):
                        e_tp = p.op("tensor", lambda t, b=b, kc=kc: t.transpose(
                            psT[:, kc * 128:(kc + 1) * 128],
                            xbf[:, b * D + kc * 128: b * D + kc * 128 + 128],
                            ident[:]), deps=[e_cast, e_id, dep_psT]
                            if kc == 0 else [])
                        tps.append(e_tp)
                    e_ev = p.op("scalar", lambda s, b=b: s.copy(
                        xT[:].rearrange("p (kc t) -> p kc t", t=T1)
                        [:, :, b * 128:(b + 1) * 128],
                        psT[:].rearrange("p (kc f) -> p kc f", f=128)),
                        deps=tps)
                    dep_psT = e_ev
                    e_xT_ev.append(e_ev)
                tl["psT_ev"] = e_xT_ev[-1]

                e_qk_ev = []
                for oc in range(16):
                    ps = psA[oc % 2]
                    mms = []
                    dep0 = e_xT_ev + wq_dep + [tl["psA_ev"][oc % 2]]
                    for kc in range(NKC):
                        e_mm = p.op("tensor", lambda t, oc=oc, kc=kc, ps=ps:
                                    t.matmul(
                            ps[:],
                            wqkv_sb[:, kc * 3 * D + oc * 128:
                                    kc * 3 * D + oc * 128 + 128],
                            xT[:, kc * T1:(kc + 1) * T1],
                            start=(kc == 0), stop=(kc == NKC - 1)),
                            deps=dep0 if kc == 0 else [])
                        mms.append(e_mm)
                    dst = q_sb if oc < 8 else k_sb
                    so = oc % 8
                    e_ev = p.op("scalar", lambda s, dst=dst, so=so, ps=ps:
                                s.copy(dst[:, so * T1:(so + 1) * T1], ps[:]),
                                deps=[mms[-1]])
                    tl["psA_ev"][oc % 2] = e_ev
                    e_qk_ev.append(e_ev)

                e_v_ev = []
                for b in range(NB1):
                    for half in range(2):
                        i = 2 + half
                        ps = psA[i]
                        mms = []
                        dep0 = e_xT_ev + wq_dep + [tl["psA_ev"][i]]
                        for kc in range(NKC):
                            e_mm = p.op("tensor", lambda t, b=b, half=half,
                                        kc=kc, ps=ps: t.matmul(
                                ps[:],
                                xT[:, kc * T1 + b * 128: kc * T1 + b * 128 + 128],
                                wqkv_sb[:, kc * 3 * D + 2 * D + half * 512:
                                        kc * 3 * D + 2 * D + half * 512 + 512],
                                start=(kc == 0), stop=(kc == NKC - 1)),
                                deps=dep0 if kc == 0 else [])
                            mms.append(e_mm)
                        e_ev = p.op("scalar", lambda s, b=b, half=half, ps=ps:
                                    s.copy(
                            v_sb[:, b * D + half * 512: b * D + half * 512 + 512],
                            ps[:]), deps=[mms[-1]])
                        tl["psA_ev"][i] = e_ev
                        e_v_ev.append(e_ev)

                e_att_ev = [None] * NB1
                dbg_att = []
                for b in range(NB1):
                    psc = psA[b % 2]
                    mms = []
                    dep0 = e_qk_ev + [tl["psA_ev"][b % 2]] + dbg_att
                    dbg_att = []
                    for kc in range(NKC):
                        e_mm = p.op("tensor", lambda t, b=b, kc=kc, psc=psc:
                                    t.matmul(
                            psc[:, 0:128],
                            q_sb[:, kc * T1 + b * 128: kc * T1 + b * 128 + 128],
                            k_sb[:, kc * T1 + b * 128: kc * T1 + b * 128 + 128],
                            start=(kc == 0), stop=(kc == NKC - 1)),
                            deps=dep0 if kc == 0 else [])
                        mms.append(e_mm)
                    e_sm = p.op("vector", lambda v, psc=psc:
                                v.scalar_tensor_tensor(
                        sm_sb[:], psc[:, 0:128], 1.0 / 32.0, mask[:],
                        ALU.mult, ALU.add), deps=[mms[-1], e_mk])
                    tl["psA_ev"][b % 2] = e_sm
                    e_nm = p.op("vector", lambda v: v.reduce_max(
                        nmax[:], sm_sb[:], axis=AX.X, negate=True),
                        hard=[e_sm])
                    e_p = p.op("scalar", lambda s: s.activation(
                        pn_sb[:], sm_sb[:], AF.Exp, bias=nmax[:, 0:1],
                        accum_out=rsum[:]), deps=[e_nm])
                    e_rr = p.op("vector", lambda v: v.tensor_scalar(
                        rsum[:], rsum[:], 1e-6, None, ALU.add), deps=[e_p])
                    e_rc2 = p.op("vector", lambda v: v.reciprocal(
                        rsum[:], rsum[:]), hard=[e_rr])
                    e_pn = p.op("scalar", lambda sc: sc.activation(
                        pn_sb[:], pn_sb[:], AF.Identity,
                        scale=rsum[:, 0:1]), deps=[e_rc2])
                    e_ptp = p.op("tensor", lambda t, b=b: t.transpose(
                        psT[:, b * 128:(b + 1) * 128], pn_sb[:], ident[:]),
                        deps=[e_pn, tl["psT_ev"]])
                    e_pte = p.op("scalar", lambda s, b=b: s.copy(
                        pT_sb[:], psT[:, b * 128:(b + 1) * 128]),
                        deps=[e_ptp])
                    tl["psT_ev"] = e_pte
                    att_mms = []
                    for dc in range(NKC):
                        ps = psA[2 + dc // 4]
                        dep = [e_pte]
                        if dc == 0:
                            dep += e_v_ev + [tl["psA_ev"][2]]
                        if dc == 4:
                            dep += [tl["psA_ev"][3]]
                        e_mm = p.op("tensor", lambda t, b=b, dc=dc, ps=ps:
                                    t.matmul(
                            ps[:, (dc % 4) * 128:(dc % 4) * 128 + 128],
                            v_sb[:, b * D + dc * 128: b * D + dc * 128 + 128],
                            pT_sb[:], start=True, stop=True), deps=dep)
                        att_mms.append(e_mm)
                    e_ae = []
                    for i in range(2):
                        e_ev = p.op("scalar", lambda s, b=b, i=i: s.copy(
                            ao_sb[:].rearrange("p (kc t) -> p kc t", t=T1)
                            [:, 4 * i:4 * i + 4, b * 128:(b + 1) * 128],
                            psA[2 + i][:].rearrange("p (c f) -> p c f", f=128)),
                            deps=att_mms)
                        tl["psA_ev"][2 + i] = e_ev
                        e_ae.append(e_ev)
                    e_att_ev[b] = e_ae
                    if dbg and tt == 0 and b == 0:
                        dbg_att = [
                            dump("sm", sm_sb[:], [e_sm], (128, 128), F32),
                            dump("pn", pn_sb[:], [e_pn], (128, 128), BF16),
                            dump("pT", pT_sb[:], [e_pte], (128, 128), BF16),
                            dump("rsum", rsum[:], [e_rc2], (128, 1), F32),
                        ]

                e_y = []
                for b in range(NB1):
                    for half in range(2):
                        i = half
                        ps = psA[i]
                        mms = []
                        dep0 = e_att_ev[b] + [tl["psA_ev"][i]] + e_wt_ev
                        for jc in range(NKC):
                            e_mm = p.op("tensor", lambda t, b=b, half=half,
                                        jc=jc, ps=ps: t.matmul(
                                ps[:],
                                ao_sb[:, jc * T1 + b * 128:
                                      jc * T1 + b * 128 + 128],
                                woutT[:, jc * D + half * 512:
                                      jc * D + half * 512 + 512],
                                start=(jc == 0), stop=(jc == NKC - 1)),
                                deps=dep0 if jc == 0 else [])
                            mms.append(e_mm)
                        e_yv = p.op("vector", lambda v, b=b, half=half, ps=ps:
                                    v.tensor_tensor(
                            y_sb[:, b * D + half * 512:
                                 b * D + half * 512 + 512],
                            ps[:],
                            xb_sb[:, b * D + half * 512:
                                  b * D + half * 512 + 512],
                            ALU.add), deps=[mms[-1], e_xb, tl["y_store"]])
                        tl["psA_ev"][i] = e_yv
                        e_y.append(e_yv)
                        p1_lastT = mms[-1]

                if dbg and tt == 0:
                    dump("xT", xT[:], e_xT_ev, (128, NKC * T1), BF16)
                    dump("q", q_sb[:], e_qk_ev, (128, NKC * T1), BF16)
                    dump("k", k_sb[:], e_qk_ev, (128, NKC * T1), BF16)
                    dump("v", v_sb[:], e_v_ev, (128, NB1 * D), BF16)
                    dump("ao", ao_sb[:], _flat(e_att_ev), (128, NKC * T1), BF16)
                    dump("y", y_sb[:], e_y, (128, NB1 * D), F32)
                    dump("xb", xb_sb[:], [e_xb], (128, NB1 * D), F32)
                e_red = p.op("vector", lambda v: v.reduce_sum(
                    rred[:], y_sb[:].rearrange("p (b f) -> p b f", f=D),
                    axis=AX.X), deps=e_y, hard=e_y[-2:])
                e_ng = p.op("vector", lambda v: v.tensor_scalar(
                    negm[:], rred[:], -1.0 / D, None, ALU.mult),
                    hard=[e_red])
                e_yc, e_sq = [], []
                dbg_dep = [dbg_tokens.get("xb"), dbg_tokens.get("y")] if (
                    dbg and tt == 0) else []
                for b in range(NB1):
                    e1 = p.op("scalar", lambda s, b=b: s.activation(
                        xb_sb[:, b * D:(b + 1) * D],
                        y_sb[:, b * D:(b + 1) * D],
                        AF.Identity, bias=negm[:, b:b + 1]),
                        deps=[e_ng] + dbg_dep)
                    e_yc.append(e1)
                for b in range(NB1):
                    e1 = p.op("scalar", lambda s, b=b: s.activation(
                        y_sb[:, b * D:(b + 1) * D],
                        xb_sb[:, b * D:(b + 1) * D],
                        AF.Square, accum_out=ssq[:, b:b + 1]),
                        deps=[e_yc[b]])
                    e_sq.append(e1)
                e_var = p.op("vector", lambda v: v.tensor_scalar(
                    rstd[:], ssq[:], 1.0 / D, 1e-5, ALU.mult, ALU.add),
                    deps=e_sq)
                e_sd = p.op("scalar", lambda s: s.activation(
                    rstd[:], rstd[:], AF.Sqrt), deps=[e_var])
                e_rs = p.op("vector", lambda v: v.reciprocal(
                    rstd[:], rstd[:]), deps=[e_sd])
                e_x2 = []
                for b in range(NB1):
                    e1 = p.op("scalar", lambda sc, b=b: sc.activation(
                        y_sb[:, b * D:(b + 1) * D],
                        xb_sb[:, b * D:(b + 1) * D],
                        AF.Identity, scale=rstd[:, b:b + 1]), deps=[e_rs])
                    e_x2.append(e1)
                tl["ln_s"] = e_sq[-1]
                e_sts = []
                for b in range(NB1):
                    e1 = p.op("sync", lambda sy, b=b, tt=tt: sy.dma_start(
                        out=x2s_d[tt * T1 + b * 128: tt * T1 + (b + 1) * 128, :],
                        in_=y_sb[:, b * D:(b + 1) * D]),
                        deps=[e_x2[b]], dma=True)
                    e_sts.append(e1)
                tl["y_store"] = e_sts[-1]
                tl["xf_free"][pb] = [e_xb, e_cast]
                tl["xbf_free"] = tps[-1] if False else e_xT_ev
                tl["xbf_free"] = e_xT_ev[-1]
                p1_lastV = e_rs
                p1_lastS = e_x2[-1]
                p1_dma = e_sts[-1]

            p1_bar = [p1_lastV, p1_lastS, p1_lastT]

        with ExitStack() as st2:
            wgqT = st2.enter_context(nc.sbuf_tensor("wgqT", [128, NKC * I_FF], BF16))
            wuqT = st2.enter_context(nc.sbuf_tensor("wuqT", [128, NKC * I_FF], BF16))
            wdqT = st2.enter_context(nc.sbuf_tensor("wdqT", [128, NIC * D], BF16))
            x2t = [st2.enter_context(nc.sbuf_tensor(f"x2t{i}", [128, D], F32))
                   for i in range(2)]
            r2 = st2.enter_context(nc.sbuf_tensor("r2", [128, I_FF], F32))
            sg2 = st2.enter_context(nc.sbuf_tensor("sg2", [128, I_FF], BF16))
            qa2 = st2.enter_context(nc.sbuf_tensor("qa2", [128, I_FF], BF16))
            qb2 = st2.enter_context(nc.sbuf_tensor("qb2", [128, I_FF], BF16))
            qc2 = st2.enter_context(nc.sbuf_tensor("qc2", [128, I_FF], BF16))
            bmax2 = st2.enter_context(nc.sbuf_tensor("bmax2", [128, I_FF // 16], F32))
            ssc2 = st2.enter_context(nc.sbuf_tensor("ssc2", [128, I_FF // 16], F32))
            rcp2 = st2.enter_context(nc.sbuf_tensor("rcp2", [128, I_FF // 16], F32))
            xqT = st2.enter_context(nc.sbuf_tensor("xqT", [128, NKC * 128], BF16))
            h_sb = st2.enter_context(nc.sbuf_tensor("h_sb", [128, I_FF], F32))
            hqT = st2.enter_context(nc.sbuf_tensor("hqT", [128, NIC * 128], BF16))
            y2 = st2.enter_context(nc.sbuf_tensor("y2", [128, D], F32))
            o_sb = st2.enter_context(nc.sbuf_tensor("o_sb", [128, D], F32))
            nm2 = st2.enter_context(nc.sbuf_tensor("nm2", [128, 1], F32))
            ss2 = st2.enter_context(nc.sbuf_tensor("ss2", [128, 1], F32))
            rst2 = st2.enter_context(nc.sbuf_tensor("rst2", [128, 1], F32))
            psg = [st2.enter_context(nc.psum_tensor(f"psg{i}", [128, 512], F32))
                   for i in range(2)]
            psu = [st2.enter_context(nc.psum_tensor(f"psu{i}", [128, 512], F32))
                   for i in range(2)]
            psd = [st2.enter_context(nc.psum_tensor(f"psd{i}", [128, 512], F32))
                   for i in range(2)]
            psT2 = st2.enter_context(nc.psum_tensor("psT2", [128, 1024], BF16))

            e_wld = []
            for src, dst in ((wgq_d, wgqT), (wuq_d, wuqT), (wdq_d, wdqT)):
                e1 = p.op("sync", lambda sy, src=src, dst=dst: sy.dma_start(
                    out=dst[:], in_=src[:, :]),
                    deps=p0_dma + [p1_dma] + p1_bar, dma=True)
                e_wld.append(e1)

            t2 = {"x2_free": [None, None], "o_store": None,
                  "psT_ev": tl["psT_ev"],
                  "psg_ev": [tl["psA_ev"][0], tl["psA_ev"][1]],
                  "psu_ev": [tl["psA_ev"][2], tl["psA_ev"][3]],
                  "psd_ev": [None, None], "r_free": None, "qa_free": None,
                  "h_free": None, "y2_s": None, "chain_v": None,
                  "sg_free": None}

            nblocks = [(0, 512), (512, 512), (1024, 512), (1536, 512),
                       (2048, 512), (2560, 256)]

            for tt in range(NT2):
                pb = tt % 2
                e_ld = p.op("sync", lambda sy, tt=tt, pb=pb: sy.dma_start(
                    out=x2t[pb][:], in_=x2s_d[tt * T2:(tt + 1) * T2, :]),
                    deps=[t2["x2_free"][pb]] + e_wld, dma=True)
                qx = emit_quant(
                    p, [e_ld, t2["r_free"], t2["qa_free"], t2["sg_free"],
                        t2["chain_v"]],
                    x2t[pb][:], D, sg2, qa2, qb2, qc2, bmax2, ssc2, rcp2,
                    r2[:, :D], qa2[:, :D])
                tps = []
                for kc in range(NKC):
                    e_tp = p.op("tensor", lambda t, kc=kc: t.transpose(
                        psT2[:, kc * 128:(kc + 1) * 128],
                        qa2[:, kc * 128:(kc + 1) * 128], ident[:]),
                        deps=[qx["q"], e_id, t2["psT_ev"]] if kc == 0 else [])
                    tps.append(e_tp)
                e_xqe = p.op("scalar", lambda s: s.copy(xqT[:], psT2[:]),
                             deps=tps)
                t2["psT_ev"] = e_xqe
                e_xq_tp_last = tps[-1]

                e_h = []
                for ps_i in range(3):
                    nb0, nb1 = nblocks[2 * ps_i], nblocks[2 * ps_i + 1]
                    g_mms, u_mms = [[], []], [[], []]
                    dep0 = [e_xqe] + e_wld + t2["psg_ev"] + t2["psu_ev"]
                    first = True
                    for kc in range(NKC):
                        for j, (off, w) in enumerate((nb0, nb1)):
                            e1 = p.op("tensor", lambda t, kc=kc, off=off, w=w,
                                      j=j: t.matmul(
                                psg[j][:, :w],
                                xqT[:, kc * 128:(kc + 1) * 128],
                                wgqT[:, kc * I_FF + off: kc * I_FF + off + w],
                                start=(kc == 0), stop=(kc == NKC - 1)),
                                deps=dep0 if first else [])
                            first = False
                            g_mms[j].append(e1)
                            e2 = p.op("tensor", lambda t, kc=kc, off=off, w=w,
                                      j=j: t.matmul(
                                psu[j][:, :w],
                                xqT[:, kc * 128:(kc + 1) * 128],
                                wuqT[:, kc * I_FF + off: kc * I_FF + off + w],
                                start=(kc == 0), stop=(kc == NKC - 1)),
                                deps=[])
                            u_mms[j].append(e2)
                    for j, (off, w) in enumerate((nb0, nb1)):
                        e_si = p.op("scalar", lambda s, off=off, w=w, j=j:
                                    s.activation(
                            h_sb[:, off:off + w], psg[j][:, :w], AF.Silu),
                            deps=[g_mms[j][-1], u_mms[j][-1], t2["h_free"]])
                        e_hm = p.op("vector", lambda v, off=off, w=w, j=j:
                                    v.tensor_tensor(
                            h_sb[:, off:off + w], h_sb[:, off:off + w],
                            psu[j][:, :w], ALU.mult), deps=[e_si])
                        t2["psg_ev"][j] = e_hm
                        t2["psu_ev"][j] = e_hm
                        e_h.append(e_hm)

                qh = emit_quant(p, e_h + [e_xq_tp_last], h_sb[:], I_FF,
                                sg2, qa2, qb2, qc2, bmax2, ssc2, rcp2,
                                r2[:], qa2[:], hard0=e_h[-2:])
                t2["chain_v"] = qh["v_last"]
                t2["h_free"] = qh["r"]
                t2["sg_free"] = qh["q"]
                e_hqe = []
                dep_psT = t2["psT_ev"]
                last_tp = None
                for b0 in range(0, NIC, 8):
                    bn = min(8, NIC - b0)
                    tps2 = []
                    for j in range(bn):
                        e_tp = p.op("tensor", lambda t, j=j, b0=b0: t.transpose(
                            psT2[:, j * 128:(j + 1) * 128],
                            qa2[:, (b0 + j) * 128:(b0 + j + 1) * 128],
                            ident[:]),
                            deps=[qh["q"], dep_psT] if j == 0 else [])
                        tps2.append(e_tp)
                    e_ev = p.op("scalar", lambda s, b0=b0, bn=bn: s.copy(
                        hqT[:, b0 * 128:(b0 + bn) * 128], psT2[:, :bn * 128]),
                        deps=tps2)
                    dep_psT = e_ev
                    e_hqe.append(e_ev)
                    last_tp = tps2[-1]
                t2["psT_ev"] = e_hqe[-1]
                t2["qa_free"] = last_tp
                t2["r_free"] = qh["v_last"]

                d_mms = [[], []]
                dep0 = e_hqe + e_wld + t2["psd_ev"]
                first = True
                for nc22 in range(NIC):
                    for j in range(2):
                        e1 = p.op("tensor", lambda t, nc22=nc22, j=j: t.matmul(
                            psd[j][:],
                            hqT[:, nc22 * 128:(nc22 + 1) * 128],
                            wdqT[:, nc22 * D + j * 512:
                                 nc22 * D + j * 512 + 512],
                            start=(nc22 == 0), stop=(nc22 == NIC - 1)),
                            deps=dep0 if first else [])
                        first = False
                        d_mms[j].append(e1)
                e_y2 = []
                for j in range(2):
                    e1 = p.op("vector", lambda v, j=j, pb=pb: v.tensor_tensor(
                        y2[:, j * 512:(j + 1) * 512], psd[j][:],
                        x2t[pb][:, j * 512:(j + 1) * 512], ALU.add),
                        deps=[d_mms[j][-1], t2["y2_s"]])
                    t2["psd_ev"][j] = e1
                    e_y2.append(e1)
                t2["x2_free"][pb] = e_y2
                e_red = p.op("vector", lambda v: v.reduce_sum(
                    nm2[:], y2[:], axis=AX.X), deps=e_y2, hard=e_y2[-2:])
                e_ng = p.op("vector", lambda v: v.tensor_scalar(
                    nm2[:], nm2[:], -1.0 / D, None, ALU.mult), hard=[e_red])
                e_yc = p.op("scalar", lambda s: s.activation(
                    r2[:, :D], y2[:], AF.Identity, bias=nm2[:, 0:1]),
                    deps=[e_ng, t2["r_free"]])
                e_sq = p.op("scalar", lambda s: s.activation(
                    y2[:], r2[:, :D], AF.Square, accum_out=ss2[:]),
                    deps=[e_yc])
                t2["y2_s"] = e_sq
                e_var = p.op("vector", lambda v: v.tensor_scalar(
                    rst2[:], ss2[:], 1.0 / D, 1e-5, ALU.mult, ALU.add),
                    deps=[e_sq])
                e_sd = p.op("scalar", lambda s: s.activation(
                    rst2[:], rst2[:], AF.Sqrt), deps=[e_var])
                e_rs = p.op("vector", lambda v: v.reciprocal(
                    rst2[:], rst2[:]), deps=[e_sd])
                e_x3 = p.op("scalar", lambda sc: sc.activation(
                    o_sb[:], r2[:, :D], AF.Identity, scale=rst2[:, 0:1]),
                    deps=[e_rs, t2["o_store"]])
                t2["r_free"] = e_x3
                e_st = p.op("sync", lambda sy, tt=tt: sy.dma_start(
                    out=out_d[tt * T2:(tt + 1) * T2, :], in_=o_sb[:]),
                    deps=[e_x3], dma=True)
                t2["o_store"] = e_st

            p.op("vector", lambda v: v.memset(nm2[:], 0.0),
                 deps=[t2["o_store"]])

        p.finalize(block, sems)
    return nc


_NC_CACHE = {}


def kernel(L0_attn_x, L0_attn_W_qkv, L0_attn_W_out, L0_attn_b_out,
           L0_ffn_w_gate, L0_ffn_w_up, L0_ffn_w_down):
    x = np.ascontiguousarray(np.asarray(L0_attn_x, dtype=np.float32))
    wqkv = np.ascontiguousarray(np.asarray(L0_attn_W_qkv, dtype=np.float32))
    wout = np.ascontiguousarray(np.asarray(L0_attn_W_out, dtype=np.float32))
    bout = np.asarray(L0_attn_b_out, dtype=np.float32)
    bfull = np.ascontiguousarray(np.broadcast_to(bout[None, :], (128, D)))
    wg = np.ascontiguousarray(np.asarray(L0_ffn_w_gate, dtype=np.float32))
    wu = np.ascontiguousarray(np.asarray(L0_ffn_w_up, dtype=np.float32))
    wd = np.ascontiguousarray(np.asarray(L0_ffn_w_down, dtype=np.float32))

    if "nc" not in _NC_CACHE:
        _NC_CACHE["nc"] = build_program()
    nc = _NC_CACHE["nc"]
    in_maps = [{"x": x[i], "wqkv": wqkv, "wout": wout, "bfull": bfull,
                "wg": wg, "wu": wu, "wd": wd} for i in range(8)]
    res = run_bass_kernel_spmd(nc, in_maps, core_ids=list(range(8)))
    return np.stack([np.asarray(r["out"]) for r in res.results], axis=0)


# version 14
# speedup vs baseline: 1.0791x; 1.0791x over previous
import numpy as np
import concourse.bass as bass
import concourse.mybir as mybir
from concourse.bass_utils import run_bass_kernel_spmd

F32 = mybir.dt.float32
BF16 = mybir.dt.bfloat16
I32 = mybir.dt.int32
AF = mybir.ActivationFunctionType
ALU = mybir.AluOpType
AX = mybir.AxisListType

S_LEN = 4096
D = 1024
I_FF = 2816
NKC = D // 128
NIC = I_FF // 128
T1 = 512
NT1 = S_LEN // T1
NB1 = T1 // 128
T2 = 128
NT2 = S_LEN // T2

ENGINES = ("sync", "gpsimd", "vector", "scalar", "tensor")


class Ev:
    __slots__ = ("eng", "emit", "deps", "hard", "dma", "needs_inc", "inc_val")

    def __init__(self, eng, emit, deps, hard, dma):
        self.eng = eng
        self.emit = emit
        self.deps = deps
        self.hard = hard
        self.dma = dma
        self.needs_inc = dma
        self.inc_val = None


def _flat(deps):
    out = []
    for d in deps:
        if d is None:
            continue
        if isinstance(d, (list, tuple)):
            out.extend(_flat(d))
        else:
            out.append(d)
    return out


class Prog:
    def __init__(self):
        self.events = {e: [] for e in ENGINES}

    def op(self, eng, emit, deps=(), dma=False, hard=()):
        ev = Ev(eng, emit, _flat(deps), _flat(hard), dma)
        self.events[eng].append(ev)
        return ev

    def finalize(self, block, sems):
        for eng in ENGINES:
            for ev in self.events[eng]:
                for d in ev.deps:
                    if d.eng != ev.eng or d.dma:
                        d.needs_inc = True
                for d in ev.hard:
                    d.needs_inc = True
        for eng in ENGINES:
            c, dc = 0, 0
            for ev in self.events[eng]:
                if ev.dma:
                    dc += 1
                    ev.inc_val = 16 * dc
                elif ev.needs_inc:
                    c += 1
                    ev.inc_val = c

        def emit_stream(eng_obj, eng_name):
            hw = {}
            for ev in self.events[eng_name]:
                for d in ev.deps + ev.hard:
                    if d not in ev.hard and d.eng == ev.eng and (
                            not d.dma or ev.dma):
                        continue
                    sname = ("d_" if d.dma else "c_") + d.eng
                    v = d.inc_val
                    if hw.get(sname, -1) >= v:
                        continue
                    eng_obj.wait_ge(sems[sname], v)
                    hw[sname] = v
                inst = ev.emit(eng_obj)
                if ev.needs_inc:
                    assert inst is not None, f"{eng_name} event needs inc"
                    sname = ("d_" if ev.dma else "c_") + ev.eng
                    inst.then_inc(sems[sname], 16 if ev.dma else 1)

        for name in ENGINES:
            getattr(block, name)(lambda eng, n=name: emit_stream(eng, n))


def emit_quant(p, deps, x_ap, F, sg, qa, qb, qc, bmax, ssc, rcp, ax_ap,
               u_bf, out_ap, hard0=(), sg_dep=(), ax_dep=(), chain_dep=(),
               out_dep=(), ssc_dep=(), rcp_dep=()):
    nb = F // 16
    b3 = x_ap.rearrange("p (s n) -> p s n", n=16)
    e_sg = p.op("scalar", lambda s_: s_.activation(sg[:, :F], x_ap, AF.Sign),
                deps=list(deps) + list(sg_dep))
    e_ax = p.op("scalar", lambda s_: s_.activation(ax_ap, x_ap, AF.Abs),
                deps=[e_sg] + list(ax_dep))
    e_bm = p.op("vector", lambda v: v.tensor_reduce(
        bmax[:, :nb], b3, axis=AX.X, op=ALU.max, apply_absolute_value=True),
        deps=deps, hard=hard0)
    e_s = p.op("vector", lambda v: v.tensor_scalar(
        ssc[:, :nb], bmax[:, :nb], 1.0 / 12.0, 5e-13, ALU.mult, ALU.max),
        deps=list(ssc_dep), hard=[e_bm])
    e_s4 = p.op("vector", lambda v: v.tensor_scalar(
        rcp[:, :nb], bmax[:, :nb], 1.0 / 24.0, 2.5e-13, ALU.mult, ALU.max),
        deps=list(rcp_dep), hard=[e_bm])
    e_rc = p.op("vector", lambda v: v.reciprocal(rcp[:, :nb], rcp[:, :nb]),
                hard=[e_s4])
    rb = rcp[:, :nb].unsqueeze(-1).broadcast_to((128, nb, 16))
    e_u = p.op("vector", lambda g: g.tensor_tensor(
        ax_ap.rearrange("p (s n) -> p s n", n=16),
        ax_ap.rearrange("p (s n) -> p s n", n=16), rb, ALU.mult),
        deps=[e_ax], hard=[e_rc])
    u = ax_ap
    e_t = p.op("vector", lambda v: v.tensor_scalar(
        qa[:, :F], u, 1.0, None, ALU.is_gt), deps=[e_u] + list(chain_dep))
    for m, dst, src in ((3.0, qb, qa), (5.0, qa, qb), (7.0, qb, qa)):
        e_t = p.op("vector", lambda v, m=m, dst=dst, src=src:
                   v.scalar_tensor_tensor(dst[:, :F], u, m, src[:, :F],
                                          ALU.is_gt, ALU.add), deps=[e_t])
    e_last = e_t
    for m, d in ((10.0, 2.0), (14.0, 2.0), (20.0, 4.0)):
        e_v = p.op("vector", lambda v, m=m, d=d: v.tensor_scalar(
            qc[:, :F], u, m, d, ALU.is_gt, ALU.mult), deps=[e_last])
        e_last = p.op("vector", lambda v: v.tensor_tensor(
            qb[:, :F], qb[:, :F], qc[:, :F], ALU.add), deps=[e_v])
    e_vlast = e_last
    e_z = p.op("vector", lambda v: v.tensor_tensor(
        qb[:, :F], qb[:, :F], sg[:, :F], ALU.mult), deps=[e_last, e_sg])
    sb = ssc[:, :nb].unsqueeze(-1).broadcast_to((128, nb, 16))
    e_q = p.op("vector", lambda g: g.tensor_tensor(
        out_ap.rearrange("p (s n) -> p s n", n=16),
        qb[:, :F].rearrange("p (s n) -> p s n", n=16), sb, ALU.mult),
        deps=[e_z] + list(out_dep))
    return {"q": e_q, "bm": e_bm, "sg": e_sg, "ax": e_ax, "u": e_u,
            "v_last": e_vlast, "z": e_z}


def build_program(dbg=False):
    scratch_kind = "ExternalOutput" if dbg else "Internal"
    nc = bass.Bass("TRN2", target_bir_lowering=False, debug=False)
    x_d = nc.dram_tensor("x", [S_LEN, D], F32, kind="ExternalInput").ap()
    wqkv_d = nc.dram_tensor("wqkv", [D, 3 * D], F32, kind="ExternalInput").ap()
    wout_d = nc.dram_tensor("wout", [D, D], F32, kind="ExternalInput").ap()
    bfull_d = nc.dram_tensor("bfull", [128, D], F32, kind="ExternalInput").ap()
    wg_d = nc.dram_tensor("wg", [I_FF, D], F32, kind="ExternalInput").ap()
    wu_d = nc.dram_tensor("wu", [I_FF, D], F32, kind="ExternalInput").ap()
    wd_d = nc.dram_tensor("wd", [D, I_FF], F32, kind="ExternalInput").ap()
    out_d = nc.dram_tensor("out", [S_LEN, D], F32, kind="ExternalOutput").ap()
    x2s_d = nc.dram_tensor("x2s", [S_LEN, D], F32, kind=scratch_kind).ap()
    wgq_d = nc.dram_tensor("wgq_s", [128, NKC * I_FF], BF16, kind=scratch_kind).ap()
    wuq_d = nc.dram_tensor("wuq_s", [128, NKC * I_FF], BF16, kind=scratch_kind).ap()
    wdq_d = nc.dram_tensor("wdq_s", [128, NIC * D], BF16, kind=scratch_kind).ap()

    p = Prog()
    dbg_tokens = {}

    def dump(name, ap, deps, shape, dt):
        if not dbg:
            return None
        dd = nc.dram_tensor("dbg_" + name, list(shape), dt,
                            kind="ExternalOutput").ap()
        ev = p.op("sync", lambda sy: sy.dma_start(out=dd[:, :], in_=ap),
                  deps=deps, dma=True)
        dbg_tokens[name] = ev
        return ev

    from contextlib import ExitStack

    with ExitStack() as st:
        sems = {}
        for e in ENGINES:
            sems["c_" + e] = st.enter_context(nc.semaphore("c_" + e))
            sems["d_" + e] = st.enter_context(nc.semaphore("d_" + e))

        iota_t = st.enter_context(nc.sbuf_tensor("iota_t", [128, 128], I32))
        ident = st.enter_context(nc.sbuf_tensor("ident", [128, 128], BF16))
        mask = st.enter_context(nc.sbuf_tensor("mask", [128, 128], F32))
        e_io = p.op("gpsimd", lambda g: g.iota(
            iota_t[:], pattern=[[1, 128]], base=0, channel_multiplier=-1))
        e_id = p.op("vector", lambda v: v.tensor_single_scalar(
            ident[:], iota_t[:], 0.0, ALU.is_equal), deps=[e_io])
        e_mk = p.op("vector", lambda v: v.tensor_scalar(
            mask[:], iota_t[:], 0.0, -3.0e38, ALU.is_gt, ALU.mult),
            deps=[e_io])

        block = st.enter_context(nc.Block())

        with ExitStack() as st0:
            wch = [st0.enter_context(nc.sbuf_tensor(f"wch{i}", [128, I_FF], F32))
                   for i in range(2)]
            r_s = st0.enter_context(nc.sbuf_tensor("r0", [128, I_FF], F32))
            sg_s = st0.enter_context(nc.sbuf_tensor("sg0", [128, I_FF], BF16))
            qa = st0.enter_context(nc.sbuf_tensor("qa0", [128, I_FF], BF16))
            qb = st0.enter_context(nc.sbuf_tensor("qb0", [128, I_FF], BF16))
            qc = st0.enter_context(nc.sbuf_tensor("qc0", [128, I_FF], BF16))
            bmax = st0.enter_context(nc.sbuf_tensor("bmax0", [128, I_FF // 16], F32))
            ssc = st0.enter_context(nc.sbuf_tensor("ssc0", [128, I_FF // 16], F32))
            rcp = st0.enter_context(nc.sbuf_tensor("rcp0", [128, I_FF // 16], F32))
            wqb = st0.enter_context(nc.sbuf_tensor("wqb0", [128, I_FF], BF16))
            u0 = None
            strips = st0.enter_context(nc.sbuf_tensor(
                "strips0", [128, NKC * I_FF], BF16))
            pst = st0.enter_context(nc.psum_tensor("pst0", [128, 1024], BF16))

            st_prev = None

            def do_weight(w_dram, n_chunks, F, n_tp, scratch_d, st_prev_dep):
                ld_reuse = do_weight.ld_reuse
                q_prev = do_weight.q_prev
                tp_prev = st_prev_dep
                last_ev = None
                prev_qt = do_weight.prev_qt
                for c in range(n_chunks):
                    buf = wch[c % 2]
                    e_ld = p.op("sync", lambda sy, b=buf, c=c: sy.dma_start(
                        out=b[:, :F], in_=w_dram[c * 128:(c + 1) * 128, :]),
                        deps=[ld_reuse[c % 2]], dma=True)
                    pq = prev_qt or {}
                    qt = emit_quant(p, [e_ld], buf[:, :F], F,
                                    sg_s, qa, qb, qc, bmax, ssc, rcp,
                                    r_s[:, :F], u0, wqb[:, :F],
                                    sg_dep=[pq.get("z")],
                                    ax_dep=[pq.get("u"), pq.get("v_last")],
                                    chain_dep=[pq.get("q")],
                                    out_dep=[q_prev],
                                    ssc_dep=[pq.get("q")],
                                    rcp_dep=[pq.get("u")])
                    prev_qt = qt
                    do_weight.prev_qt = qt
                    ld_reuse[c % 2] = [qt["bm"], qt["ax"]]
                    do_weight.ld_reuse = ld_reuse
                    ev_list = []
                    for b0 in range(0, n_tp, 8):
                        bn = min(8, n_tp - b0)
                        tps = []
                        for j in range(bn):
                            e_tp = p.op("tensor", lambda t, j=j, b0=b0: t.transpose(
                                pst[:, j * 128:(j + 1) * 128],
                                wqb[:, (b0 + j) * 128:(b0 + j + 1) * 128],
                                ident[:]), deps=[qt["q"], e_id, tp_prev])
                            tps.append(e_tp)
                        e_ev = p.op("scalar", lambda s, b0=b0, bn=bn, c=c,
                                    n_chunks=n_chunks: s.copy(
                            strips[:].rearrange("p (t f) -> p t f",
                                                f=n_chunks * 128)
                            [:, b0:b0 + bn, c * 128:(c + 1) * 128],
                            pst[:, :bn * 128].rearrange("p (t f) -> p t f",
                                                        f=128)), deps=tps)
                        tp_prev = e_ev
                        ev_list.append(e_ev)
                        q_prev = tps[-1]
                        do_weight.q_prev = q_prev
                    last_ev = ev_list[-1]
                e_st = p.op("sync", lambda sy: sy.dma_start(
                    out=scratch_d[:, :n_tp * n_chunks * 128],
                    in_=strips[:, :n_tp * n_chunks * 128]),
                    deps=[last_ev], dma=True)
                return e_st, q_prev, last_ev

            do_weight.prev_qt = None
            do_weight.ld_reuse = [None, None]
            do_weight.q_prev = None
            e_g, _, _ = do_weight(wg_d, NIC, D, NKC, wgq_d, None)
            e_u_, _, _ = do_weight(wu_d, NIC, D, NKC, wuq_d, e_g)
            e_dw, p0_lastT, p0_lastS = do_weight(wd_d, NKC, I_FF, NIC, wdq_d,
                                                 e_u_)
            p0_dma = [e_g, e_u_, e_dw]

        with ExitStack() as st1:
            wqkv_sb = st1.enter_context(
                nc.sbuf_tensor("wqkv_sb", [128, NKC * 3 * D], BF16))
            woutT = st1.enter_context(nc.sbuf_tensor("woutT", [128, NKC * D], BF16))
            bfull = st1.enter_context(nc.sbuf_tensor("bfull_sb", [128, D], F32))
            with ExitStack() as st1a:
                woutb = st1a.enter_context(
                    nc.sbuf_tensor("woutb", [128, NKC * D], BF16))
                pstw = st1a.enter_context(nc.psum_tensor("pstw", [128, 1024], BF16))
                e_wq_ld = []
                for kc in range(NKC):
                    e1 = p.op("gpsimd", lambda g, kc=kc: g.dma_start(
                        out=wqkv_sb[:, kc * 3 * D:(kc + 1) * 3 * D],
                        in_=wqkv_d[kc * 128:(kc + 1) * 128, :]),
                        deps=p0_dma, dma=True)
                    e2 = p.op("gpsimd", lambda g, kc=kc: g.dma_start(
                        out=woutb[:, kc * D:(kc + 1) * D],
                        in_=wout_d[kc * 128:(kc + 1) * 128, :]),
                        deps=p0_dma, dma=True)
                    e_wq_ld += [e1, e2]
                e_bf_ld = p.op("sync", lambda sy: sy.dma_start(
                    out=bfull[:], in_=bfull_d[:, :]), deps=p0_dma, dma=True)
                prev_ev = p0_lastS
                e_wt_ev = []
                for jc in range(NKC):
                    tps = []
                    for ic in range(NKC):
                        e_tp = p.op("tensor", lambda t, ic=ic, jc=jc: t.transpose(
                            pstw[:, ic * 128:(ic + 1) * 128],
                            woutb[:, ic * D + jc * 128: ic * D + jc * 128 + 128],
                            ident[:]),
                            deps=e_wq_ld + [e_id, prev_ev, p0_lastT]
                            if ic == 0 and jc == 0 else
                            (e_wq_ld + [prev_ev] if ic == 0 else []))
                        tps.append(e_tp)
                    e_ev = p.op("scalar", lambda s, jc=jc: s.copy(
                        woutT[:, jc * D:(jc + 1) * D], pstw[:]), deps=tps)
                    prev_ev = e_ev
                    e_wt_ev.append(e_ev)

            xf = [st1.enter_context(nc.sbuf_tensor(f"xf{i}", [128, NB1 * D], F32))
                  for i in range(2)]
            xbf = st1.enter_context(nc.sbuf_tensor("xbf", [128, NB1 * D], BF16))
            xT = st1.enter_context(nc.sbuf_tensor("xT", [128, NKC * T1], BF16))
            q_sb = st1.enter_context(nc.sbuf_tensor("q_sb", [128, NKC * T1], BF16))
            k_sb = st1.enter_context(nc.sbuf_tensor("k_sb", [128, NKC * T1], BF16))
            v_sb = st1.enter_context(nc.sbuf_tensor("v_sb", [128, NB1 * D], BF16))
            ao_sb = st1.enter_context(nc.sbuf_tensor("ao_sb", [128, NKC * T1], BF16))
            xb_sb = st1.enter_context(nc.sbuf_tensor("xb_sb", [128, NB1 * D], F32))
            y_sb = st1.enter_context(nc.sbuf_tensor("y_sb", [128, NB1 * D], F32))
            sm_sb = [st1.enter_context(nc.sbuf_tensor(f"sm_sb{i}", [128, 128], F32)) for i in range(2)]
            pn_sb = [st1.enter_context(nc.sbuf_tensor(f"pn_sb{i}", [128, 128], BF16)) for i in range(4)]
            pT_sb = [st1.enter_context(nc.sbuf_tensor(f"pT_sb{i}", [128, 128], BF16)) for i in range(2)]
            nmax = [st1.enter_context(nc.sbuf_tensor(f"nmax{i}", [128, 1], F32)) for i in range(2)]
            rsum = [st1.enter_context(nc.sbuf_tensor(f"rsum{i}", [128, 1], F32)) for i in range(2)]
            rred = st1.enter_context(nc.sbuf_tensor("rred", [128, NB1], F32))
            negm = st1.enter_context(nc.sbuf_tensor("negm", [128, NB1], F32))
            ssq = st1.enter_context(nc.sbuf_tensor("ssq", [128, NB1], F32))
            rstd = st1.enter_context(nc.sbuf_tensor("rstd", [128, NB1], F32))
            psA = [st1.enter_context(nc.psum_tensor(f"psA{i}", [128, 512], F32))
                   for i in range(4)]
            psT = st1.enter_context(nc.psum_tensor("psT1", [128, 1024], BF16))

            wq_dep = e_wq_ld + [e_bf_ld, e_mk]
            tl = {"xf_free": [None, None], "y_store": None, "xbf_free": None,
                  "psA_ev": [None] * 4, "psT_ev": e_wt_ev[-1], "ln_s": None,
                  "sm_prev": [None, None], "pn_prev": [None] * 4}
            p1_lastV = p1_lastS = p1_lastT = None
            p1_dma = None

            dbg_att = []
            for tt in range(NT1):
                pb = tt % 2
                e_lds = []
                for b in range(NB1):
                    e = p.op("sync", lambda sy, b=b, tt=tt, pb=pb: sy.dma_start(
                        out=xf[pb][:, b * D:(b + 1) * D],
                        in_=x_d[tt * T1 + b * 128: tt * T1 + (b + 1) * 128, :]),
                        deps=[tl["xf_free"][pb]] +
                             ([e_wt_ev[-1]] if tt < 2 else []), dma=True)
                    e_lds.append(e)
                e_cast = p.op("scalar", lambda s, pb=pb: s.copy(
                    xbf[:], xf[pb][:]), deps=e_lds + [tl["xbf_free"]])
                e_xb = p.op("vector", lambda v, pb=pb: v.tensor_tensor(
                    xb_sb[:].rearrange("p (b f) -> p b f", f=D),
                    xf[pb][:].rearrange("p (b f) -> p b f", f=D),
                    bfull[:].unsqueeze(1).broadcast_to((128, NB1, D)), ALU.add),
                    deps=e_lds + [e_bf_ld, tl["ln_s"]])

                e_xT_ev = []
                dep_psT = tl["psT_ev"]
                for b in range(NB1):
                    tps = []
                    for kc in range(NKC):
                        e_tp = p.op("tensor", lambda t, b=b, kc=kc: t.transpose(
                            psT[:, kc * 128:(kc + 1) * 128],
                            xbf[:, b * D + kc * 128: b * D + kc * 128 + 128],
                            ident[:]), deps=[e_cast, e_id, dep_psT]
                            if kc == 0 else [])
                        tps.append(e_tp)
                    e_ev = p.op("scalar", lambda s, b=b: s.copy(
                        xT[:].rearrange("p (kc t) -> p kc t", t=T1)
                        [:, :, b * 128:(b + 1) * 128],
                        psT[:].rearrange("p (kc f) -> p kc f", f=128)),
                        deps=tps)
                    dep_psT = e_ev
                    e_xT_ev.append(e_ev)
                tl["psT_ev"] = e_xT_ev[-1]

                e_qk_ev = []
                for oc in range(16):
                    ps = psA[oc % 2]
                    mms = []
                    dep0 = e_xT_ev + wq_dep + [tl["psA_ev"][oc % 2]]
                    for kc in range(NKC):
                        e_mm = p.op("tensor", lambda t, oc=oc, kc=kc, ps=ps:
                                    t.matmul(
                            ps[:],
                            wqkv_sb[:, kc * 3 * D + oc * 128:
                                    kc * 3 * D + oc * 128 + 128],
                            xT[:, kc * T1:(kc + 1) * T1],
                            start=(kc == 0), stop=(kc == NKC - 1)),
                            deps=dep0 if kc == 0 else [])
                        mms.append(e_mm)
                    dst = q_sb if oc < 8 else k_sb
                    so = oc % 8
                    e_ev = p.op("scalar", lambda s, dst=dst, so=so, ps=ps:
                                s.copy(dst[:, so * T1:(so + 1) * T1], ps[:]),
                                deps=[mms[-1]])
                    tl["psA_ev"][oc % 2] = e_ev
                    e_qk_ev.append(e_ev)

                e_v_ev = []
                for b in range(NB1):
                    for half in range(2):
                        i = 2 + half
                        ps = psA[i]
                        mms = []
                        dep0 = e_xT_ev + wq_dep + [tl["psA_ev"][i]]
                        for kc in range(NKC):
                            e_mm = p.op("tensor", lambda t, b=b, half=half,
                                        kc=kc, ps=ps: t.matmul(
                                ps[:],
                                xT[:, kc * T1 + b * 128: kc * T1 + b * 128 + 128],
                                wqkv_sb[:, kc * 3 * D + 2 * D + half * 512:
                                        kc * 3 * D + 2 * D + half * 512 + 512],
                                start=(kc == 0), stop=(kc == NKC - 1)),
                                deps=dep0 if kc == 0 else [])
                            mms.append(e_mm)
                        e_ev = p.op("scalar", lambda s, b=b, half=half, ps=ps:
                                    s.copy(
                            v_sb[:, b * D + half * 512: b * D + half * 512 + 512],
                            ps[:]), deps=[mms[-1]])
                        tl["psA_ev"][i] = e_ev
                        e_v_ev.append(e_ev)

                e_att_ev = [None] * NB1
                sc_done = []
                for b in range(NB1):
                    psc = psA[b]
                    mms = []
                    dep0 = e_qk_ev + [tl["psA_ev"][b]] + (dbg_att if b == 0 else [])
                    for kc in range(NKC):
                        e_mm = p.op("tensor", lambda t, b=b, kc=kc, psc=psc:
                                    t.matmul(
                            psc[:, 0:128],
                            q_sb[:, kc * T1 + b * 128: kc * T1 + b * 128 + 128],
                            k_sb[:, kc * T1 + b * 128: kc * T1 + b * 128 + 128],
                            start=(kc == 0), stop=(kc == NKC - 1)),
                            deps=dep0 if kc == 0 else [])
                        mms.append(e_mm)
                    sc_done.append(mms[-1])
                dbg_att = []
                sm_prev = [tl["sm_prev"][0], tl["sm_prev"][1]]
                pn_prev = list(tl["pn_prev"])
                e_sm_l, e_pn_l = [], []
                for b in range(NB1):
                    i = b % 2
                    e_sm = p.op("vector", lambda v, b=b, i=i:
                                v.scalar_tensor_tensor(
                        sm_sb[i][:], psA[b][:, 0:128], 1.0 / 32.0, mask[:],
                        ALU.mult, ALU.add),
                        deps=[sc_done[b], e_mk, sm_prev[i]])
                    sm_prev[i] = e_sm
                    tl["psA_ev"][b] = e_sm
                    e_sm_l.append(e_sm)
                    e_nm = p.op("vector", lambda v, i=i: v.reduce_max(
                        nmax[i][:], sm_sb[i][:], axis=AX.X, negate=True),
                        hard=[e_sm])
                    e_p = p.op("scalar", lambda s, b=b, i=i: s.activation(
                        pn_sb[b][:], sm_sb[i][:], AF.Exp, bias=nmax[i][:, 0:1],
                        accum_out=rsum[i][:]), deps=[e_nm, pn_prev[b]])
                    e_rr = p.op("vector", lambda v, i=i: v.tensor_scalar(
                        rsum[i][:], rsum[i][:], 1e-6, None, ALU.add),
                        deps=[e_p])
                    e_rc2 = p.op("vector", lambda v, i=i: v.reciprocal(
                        rsum[i][:], rsum[i][:]), hard=[e_rr])
                    e_pn = p.op("scalar", lambda sc, b=b, i=i: sc.activation(
                        pn_sb[b][:], pn_sb[b][:], AF.Identity,
                        scale=rsum[i][:, 0:1]), deps=[e_rc2])
                    e_pn_l.append(e_pn)
                pair_free = {0: [e_sm_l[0], e_sm_l[1]],
                             1: [e_sm_l[2], e_sm_l[3]]}
                for b in range(NB1):
                    i = b % 2
                    e_ptp = p.op("tensor", lambda t, b=b: t.transpose(
                        psT[:, b * 128:(b + 1) * 128], pn_sb[b][:], ident[:]),
                        deps=[e_pn_l[b], tl["psT_ev"]])
                    e_pte = p.op("scalar", lambda s, b=b, i=i: s.copy(
                        pT_sb[i][:], psT[:, b * 128:(b + 1) * 128]),
                        deps=[e_ptp])
                    tl["psT_ev"] = e_pte
                    pn_prev[b] = e_ptp
                    pg = b % 2
                    att_mms = []
                    for dc in range(NKC):
                        ps = psA[2 * pg + dc // 4]
                        dep = [e_pte]
                        if dc == 0:
                            dep += e_v_ev + pair_free[pg]
                        e_mm = p.op("tensor", lambda t, b=b, dc=dc, ps=ps,
                                    i=i: t.matmul(
                            ps[:, (dc % 4) * 128:(dc % 4) * 128 + 128],
                            v_sb[:, b * D + dc * 128: b * D + dc * 128 + 128],
                            pT_sb[i][:], start=True, stop=True), deps=dep)
                        att_mms.append(e_mm)
                    e_ae = []
                    for k2 in range(2):
                        e_ev = p.op("scalar", lambda s, b=b, k2=k2, pg=pg: s.copy(
                            ao_sb[:].rearrange("p (kc t) -> p kc t", t=T1)
                            [:, 4 * k2:4 * k2 + 4, b * 128:(b + 1) * 128],
                            psA[2 * pg + k2][:].rearrange(
                                "p (c f) -> p c f", f=128)),
                            deps=att_mms)
                        tl["psA_ev"][2 * pg + k2] = e_ev
                        e_ae.append(e_ev)
                    pair_free[pg] = e_ae
                    e_att_ev[b] = e_ae
                tl["sm_prev"] = sm_prev
                tl["pn_prev"] = pn_prev
                e_y = []
                for b in range(NB1):
                    for half in range(2):
                        i = half
                        ps = psA[i]
                        mms = []
                        dep0 = e_att_ev[b] + [tl["psA_ev"][i]] + e_wt_ev
                        for jc in range(NKC):
                            e_mm = p.op("tensor", lambda t, b=b, half=half,
                                        jc=jc, ps=ps: t.matmul(
                                ps[:],
                                ao_sb[:, jc * T1 + b * 128:
                                      jc * T1 + b * 128 + 128],
                                woutT[:, jc * D + half * 512:
                                      jc * D + half * 512 + 512],
                                start=(jc == 0), stop=(jc == NKC - 1)),
                                deps=dep0 if jc == 0 else [])
                            mms.append(e_mm)
                        e_yv = p.op("vector", lambda v, b=b, half=half, ps=ps:
                                    v.tensor_tensor(
                            y_sb[:, b * D + half * 512:
                                 b * D + half * 512 + 512],
                            ps[:],
                            xb_sb[:, b * D + half * 512:
                                  b * D + half * 512 + 512],
                            ALU.add), deps=[mms[-1], e_xb, tl["y_store"]])
                        tl["psA_ev"][i] = e_yv
                        e_y.append(e_yv)
                        p1_lastT = mms[-1]

                if dbg and tt == 0:
                    dump("xT", xT[:], e_xT_ev, (128, NKC * T1), BF16)
                    dump("q", q_sb[:], e_qk_ev, (128, NKC * T1), BF16)
                    dump("k", k_sb[:], e_qk_ev, (128, NKC * T1), BF16)
                    dump("v", v_sb[:], e_v_ev, (128, NB1 * D), BF16)
                    dump("ao", ao_sb[:], _flat(e_att_ev), (128, NKC * T1), BF16)
                    dump("y", y_sb[:], e_y, (128, NB1 * D), F32)
                    dump("xb", xb_sb[:], [e_xb], (128, NB1 * D), F32)
                e_red = p.op("vector", lambda v: v.reduce_sum(
                    rred[:], y_sb[:].rearrange("p (b f) -> p b f", f=D),
                    axis=AX.X), deps=e_y, hard=e_y[-2:])
                e_ng = p.op("vector", lambda v: v.tensor_scalar(
                    negm[:], rred[:], -1.0 / D, None, ALU.mult),
                    hard=[e_red])
                e_yc, e_sq = [], []
                dbg_dep = [dbg_tokens.get("xb"), dbg_tokens.get("y")] if (
                    dbg and tt == 0) else []
                for b in range(NB1):
                    e1 = p.op("scalar", lambda s, b=b: s.activation(
                        xb_sb[:, b * D:(b + 1) * D],
                        y_sb[:, b * D:(b + 1) * D],
                        AF.Identity, bias=negm[:, b:b + 1]),
                        deps=[e_ng] + dbg_dep)
                    e_yc.append(e1)
                for b in range(NB1):
                    e1 = p.op("scalar", lambda s, b=b: s.activation(
                        y_sb[:, b * D:(b + 1) * D],
                        xb_sb[:, b * D:(b + 1) * D],
                        AF.Square, accum_out=ssq[:, b:b + 1]),
                        deps=[e_yc[b]])
                    e_sq.append(e1)
                e_var = p.op("vector", lambda v: v.tensor_scalar(
                    negm[:], ssq[:], 1.0 / D, 1e-5, ALU.mult, ALU.add),
                    deps=e_sq)
                e_sd = p.op("scalar", lambda s: s.activation(
                    ssq[:], negm[:], AF.Ln), deps=[e_var])
                e_rs = p.op("scalar", lambda s: s.activation(
                    rstd[:], ssq[:], AF.Exp, scale=-0.5), hard=[e_sd])
                e_x2 = []
                for b in range(NB1):
                    e1 = p.op("scalar", lambda sc, b=b: sc.activation(
                        y_sb[:, b * D:(b + 1) * D],
                        xb_sb[:, b * D:(b + 1) * D],
                        AF.Identity, scale=rstd[:, b:b + 1]), hard=[e_rs])
                    e_x2.append(e1)
                tl["ln_s"] = e_sq[-1]
                e_sts = []
                for b in range(NB1):
                    e1 = p.op("sync", lambda sy, b=b, tt=tt: sy.dma_start(
                        out=x2s_d[tt * T1 + b * 128: tt * T1 + (b + 1) * 128, :],
                        in_=y_sb[:, b * D:(b + 1) * D]),
                        deps=[e_x2[b]], dma=True)
                    e_sts.append(e1)
                tl["y_store"] = e_sts[-1]
                tl["xf_free"][pb] = [e_xb, e_cast]
                tl["xbf_free"] = tps[-1] if False else e_xT_ev
                tl["xbf_free"] = e_xT_ev[-1]
                p1_lastV = e_rs
                p1_lastS = e_x2[-1]
                p1_dma = e_sts[-1]

            p1_bar = [p1_lastV, p1_lastS, p1_lastT]

        with ExitStack() as st2:
            wgqT = st2.enter_context(nc.sbuf_tensor("wgqT", [128, NKC * I_FF], BF16))
            wuqT = st2.enter_context(nc.sbuf_tensor("wuqT", [128, NKC * I_FF], BF16))
            wdqT = st2.enter_context(nc.sbuf_tensor("wdqT", [128, NIC * D], BF16))
            x2t = [st2.enter_context(nc.sbuf_tensor(f"x2t{i}", [128, D], F32))
                   for i in range(2)]
            r2 = st2.enter_context(nc.sbuf_tensor("r2", [128, I_FF], F32))
            sg2 = st2.enter_context(nc.sbuf_tensor("sg2", [128, I_FF], BF16))
            qa2 = st2.enter_context(nc.sbuf_tensor("qa2", [128, I_FF], BF16))
            qb2 = st2.enter_context(nc.sbuf_tensor("qb2", [128, I_FF], BF16))
            qc2 = st2.enter_context(nc.sbuf_tensor("qc2", [128, I_FF], BF16))
            bmax2 = st2.enter_context(nc.sbuf_tensor("bmax2", [128, I_FF // 16], F32))
            ssc2 = st2.enter_context(nc.sbuf_tensor("ssc2", [128, I_FF // 16], F32))
            rcp2 = st2.enter_context(nc.sbuf_tensor("rcp2", [128, I_FF // 16], F32))
            u2 = None
            rx = st2.enter_context(nc.sbuf_tensor("rx", [128, D], F32))
            qxa = st2.enter_context(nc.sbuf_tensor("qxa", [128, D], BF16))
            bmx = st2.enter_context(nc.sbuf_tensor("bmx", [128, D // 16], F32))
            ssx = st2.enter_context(nc.sbuf_tensor("ssx", [128, D // 16], F32))
            rcx = st2.enter_context(nc.sbuf_tensor("rcx", [128, D // 16], F32))
            xqT = [st2.enter_context(nc.sbuf_tensor(f"xqT{i}", [128, NKC * 128],
                                                    BF16)) for i in range(2)]
            hqT = st2.enter_context(nc.sbuf_tensor("hqT", [128, NIC * 128], BF16))
            y2 = st2.enter_context(nc.sbuf_tensor("y2", [128, D], F32))
            o_sb = st2.enter_context(nc.sbuf_tensor("o_sb", [128, D], F32))
            nm2 = st2.enter_context(nc.sbuf_tensor("nm2", [128, 1], F32))
            ss2 = st2.enter_context(nc.sbuf_tensor("ss2", [128, 1], F32))
            rst2 = st2.enter_context(nc.sbuf_tensor("rst2", [128, 1], F32))
            psg = [st2.enter_context(nc.psum_tensor(f"psg{i}", [128, 512], F32))
                   for i in range(2)]
            psu = [st2.enter_context(nc.psum_tensor(f"psu{i}", [128, 512], F32))
                   for i in range(2)]
            psd = [st2.enter_context(nc.psum_tensor(f"psd{i}", [128, 512], F32))
                   for i in range(2)]
            psT2 = st2.enter_context(nc.psum_tensor("psT2", [128, 1024], BF16))

            if dbg:
                dxq_d = nc.dram_tensor("dbg_xq", [S_LEN, D], BF16,
                                       kind="ExternalOutput").ap()
                dhq_d = nc.dram_tensor("dbg_hq", [S_LEN, I_FF], BF16,
                                       kind="ExternalOutput").ap()
                dy2_d = nc.dram_tensor("dbg_y2", [S_LEN, D], F32,
                                       kind="ExternalOutput").ap()
                drs_d = nc.dram_tensor("dbg_rs", [S_LEN, 1], F32,
                                       kind="ExternalOutput").ap()
            e_wld = []
            for src, dst in ((wgq_d, wgqT), (wuq_d, wuqT), (wdq_d, wdqT)):
                e1 = p.op("sync", lambda sy, src=src, dst=dst: sy.dma_start(
                    out=dst[:], in_=src[:, :]),
                    deps=p0_dma + [p1_dma] + p1_bar, dma=True)
                e_wld.append(e1)

            t2 = {"x2_free": [None, None], "o_store": None,
                  "psT_ev": tl["psT_ev"],
                  "psg_ev": [tl["psA_ev"][0], tl["psA_ev"][1]],
                  "psu_ev": [tl["psA_ev"][2], tl["psA_ev"][3]],
                  "psd_ev": [None, None], "r_free": None,
                  "h_free": None, "y2_s": None,
                  "xq_tp_any": None,
                  "xqT_ev": [None, None],
                  "gu_last": [None, None],
                  "qx_prev": None, "hq_tp_any": None, "qh_prev": None}

            nblocks = [(0, 512), (512, 512), (1024, 512), (1536, 512),
                       (2048, 512), (2560, 256)]

            def emit_xquant_v(tt, qh_prev):
                pb = tt % 2
                e_ld = p.op("sync", lambda sy, tt=tt, pb=pb: sy.dma_start(
                    out=x2t[pb][:], in_=x2s_d[tt * T2:(tt + 1) * T2, :]),
                    deps=[t2["x2_free"][pb]] + e_wld, dma=True)
                hp = qh_prev or {}
                xp = t2["qx_prev"] or {}
                return emit_quant(
                    p, [e_ld], x2t[pb][:], D, sg2, qxa, qb2, qc2,
                    bmx, ssx, rcx, rx[:], u2, qxa[:],
                    sg_dep=[hp.get("z")],
                    ax_dep=[xp.get("u"), xp.get("v_last")],
                    chain_dep=[hp.get("q"), t2["xq_tp_any"]],
                    out_dep=[t2["xq_tp_any"]],
                    ssc_dep=[xp.get("q")], rcp_dep=[xp.get("u")])

            def emit_xquant_t(tt, qx):
                pb = tt % 2
                tps = []
                for kc in range(NKC):
                    e_tp = p.op("tensor", lambda t, kc=kc: t.transpose(
                        psT2[:, kc * 128:(kc + 1) * 128],
                        qxa[:, kc * 128:(kc + 1) * 128], ident[:]),
                        deps=[qx["q"], e_id, t2["psT_ev"]] if kc == 0 else [])
                    tps.append(e_tp)
                e_xqe = p.op("scalar", lambda s, pb=pb: s.copy(
                    xqT[pb][:], psT2[:]), deps=tps + [t2["gu_last"][pb]])
                t2["psT_ev"] = e_xqe
                t2["xq_tp_any"] = [tps[-1]]
                if dbg:
                    t2["xq_tp_any"].append(p.op("sync", lambda sy, tt=tt:
                        sy.dma_start(out=dxq_d[tt * T2:(tt + 1) * T2, :],
                                     in_=qxa[:]), deps=[qx["q"]], dma=True))
                t2["xqT_ev"][pb] = e_xqe

            for tt in range(NT2):
                pb = tt % 2
                qx = emit_xquant_v(tt, t2["qh_prev"])
                t2["qx_prev"] = qx
                emit_xquant_t(tt, qx)
                e_h = []
                for ps_i in range(3):
                    nb0, nb1 = nblocks[2 * ps_i], nblocks[2 * ps_i + 1]
                    g_mms, u_mms = [[], []], [[], []]
                    dep0 = [t2["xqT_ev"][pb]] + e_wld + t2["psg_ev"] + t2["psu_ev"]
                    first = True
                    for kc in range(NKC):
                        for j, (off, w) in enumerate((nb0, nb1)):
                            e1 = p.op("tensor", lambda t, kc=kc, off=off, w=w,
                                      j=j, pb=pb: t.matmul(
                                psg[j][:, :w],
                                xqT[pb][:, kc * 128:(kc + 1) * 128],
                                wgqT[:, kc * I_FF + off: kc * I_FF + off + w],
                                start=(kc == 0), stop=(kc == NKC - 1)),
                                deps=dep0 if first else [])
                            first = False
                            g_mms[j].append(e1)
                            e2 = p.op("tensor", lambda t, kc=kc, off=off, w=w,
                                      j=j, pb=pb: t.matmul(
                                psu[j][:, :w],
                                xqT[pb][:, kc * 128:(kc + 1) * 128],
                                wuqT[:, kc * I_FF + off: kc * I_FF + off + w],
                                start=(kc == 0), stop=(kc == NKC - 1)),
                                deps=[])
                            u_mms[j].append(e2)
                    for j, (off, w) in enumerate((nb0, nb1)):
                        e_si = p.op("scalar", lambda s, off=off, w=w, j=j:
                                    s.activation(
                            r2[:, off:off + w], psg[j][:, :w], AF.Silu),
                            deps=[g_mms[j][-1], u_mms[j][-1], t2["h_free"]])
                        e_hm = p.op("vector", lambda v, off=off, w=w, j=j:
                                    v.tensor_tensor(
                            r2[:, off:off + w], r2[:, off:off + w],
                            psu[j][:, :w], ALU.mult), deps=[e_si])
                        t2["psg_ev"][j] = e_hm
                        t2["psu_ev"][j] = e_hm
                        e_h.append(e_hm)
                t2["gu_last"][pb] = u_mms[1][-1]

                qh = emit_quant(p, e_h, r2[:], I_FF,
                                sg2, qa2, qb2, qc2, bmax2, ssc2, rcp2,
                                r2[:], u2, qa2[:], hard0=e_h[-2:],
                                sg_dep=[qx["z"]],
                                chain_dep=[qx["q"], t2["hq_tp_any"]],
                                out_dep=[t2["hq_tp_any"]],
                                ssc_dep=[t2["qh_prev"].get("q") if t2["qh_prev"] else None],
                                rcp_dep=[t2["qh_prev"].get("u") if t2["qh_prev"] else None])
                t2["qh_prev"] = qh
                t2["h_free"] = [qh["bm"], qh["u"], qh["v_last"]]
                if dbg:
                    t2["hq_dump"] = p.op("sync", lambda sy, tt=tt:
                        sy.dma_start(out=dhq_d[tt * T2:(tt + 1) * T2, :],
                                     in_=qa2[:]), deps=[qh["q"]], dma=True)



                e_hqe = []
                dep_psT = t2["psT_ev"]
                last_tp = None
                for b0 in range(0, NIC, 8):
                    bn = min(8, NIC - b0)
                    tps2 = []
                    for j in range(bn):
                        e_tp = p.op("tensor", lambda t, j=j, b0=b0: t.transpose(
                            psT2[:, j * 128:(j + 1) * 128],
                            qa2[:, (b0 + j) * 128:(b0 + j + 1) * 128],
                            ident[:]),
                            deps=[qh["q"], dep_psT] if j == 0 else [])
                        tps2.append(e_tp)
                    e_ev = p.op("scalar", lambda s, b0=b0, bn=bn: s.copy(
                        hqT[:, b0 * 128:(b0 + bn) * 128], psT2[:, :bn * 128]),
                        deps=tps2)
                    dep_psT = e_ev
                    e_hqe.append(e_ev)
                    last_tp = tps2[-1]
                t2["psT_ev"] = e_hqe[-1]

                d_mms = [[], []]
                dep0 = e_hqe + e_wld + t2["psd_ev"]
                first = True
                for nc22 in range(NIC):
                    for j in range(2):
                        e1 = p.op("tensor", lambda t, nc22=nc22, j=j: t.matmul(
                            psd[j][:],
                            hqT[:, nc22 * 128:(nc22 + 1) * 128],
                            wdqT[:, nc22 * D + j * 512:
                                 nc22 * D + j * 512 + 512],
                            start=(nc22 == 0), stop=(nc22 == NIC - 1)),
                            deps=dep0 if first else [])
                        first = False
                        d_mms[j].append(e1)

                e_y2 = []
                for j in range(2):
                    e1 = p.op("vector", lambda v, j=j, pb=pb: v.tensor_tensor(
                        y2[:, j * 512:(j + 1) * 512], psd[j][:],
                        x2t[pb][:, j * 512:(j + 1) * 512], ALU.add),
                        deps=[d_mms[j][-1], t2["y2_s"]])
                    t2["psd_ev"][j] = e1
                    e_y2.append(e1)
                t2["x2_free"][pb] = e_y2 + [qx["ax"]]
                if dbg:
                    t2["y2_dump"] = p.op("sync", lambda sy, tt=tt:
                        sy.dma_start(out=dy2_d[tt * T2:(tt + 1) * T2, :],
                                     in_=y2[:]), deps=e_y2, dma=True)
                e_red = p.op("vector", lambda v: v.reduce_sum(
                    nm2[:], y2[:], axis=AX.X), deps=e_y2, hard=e_y2[-2:])
                e_ng = p.op("vector", lambda v: v.tensor_scalar(
                    nm2[:], nm2[:], -1.0 / D, None, ALU.mult), hard=[e_red])
                e_yc = p.op("scalar", lambda s: s.activation(
                    o_sb[:], y2[:], AF.Identity, bias=nm2[:, 0:1]),
                    deps=[e_ng, t2["o_store"]])
                e_sq = p.op("scalar", lambda s: s.activation(
                    y2[:], o_sb[:], AF.Square, accum_out=ss2[:]),
                    deps=[e_yc, t2.get("y2_dump")])
                t2["y2_s"] = e_sq
                e_var = p.op("vector", lambda v: v.tensor_scalar(
                    nm2[:], ss2[:], 1.0 / D, 1e-5, ALU.mult, ALU.add),
                    deps=[e_sq])
                e_sd = p.op("scalar", lambda s: s.activation(
                    ss2[:], nm2[:], AF.Ln), deps=[e_var])
                e_rs = p.op("scalar", lambda s: s.activation(
                    rst2[:], ss2[:], AF.Exp, scale=-0.5),
                    deps=[t2.get("rs_dump")], hard=[e_sd])
                if dbg:
                    t2["rs_dump"] = p.op("sync", lambda sy, tt=tt:
                        sy.dma_start(out=drs_d[tt * T2:(tt + 1) * T2, :],
                                     in_=rst2[:]), deps=[e_rs], dma=True)
                e_x3 = p.op("scalar", lambda sc: sc.activation(
                    o_sb[:], o_sb[:], AF.Identity, scale=rst2[:, 0:1]),
                    hard=[e_rs])
                e_st = p.op("sync", lambda sy, tt=tt: sy.dma_start(
                    out=out_d[tt * T2:(tt + 1) * T2, :], in_=o_sb[:]),
                    deps=[e_x3], dma=True)
                t2["o_store"] = e_st

            p.op("vector", lambda v: v.memset(nm2[:], 0.0),
                 deps=[t2["o_store"]])

        p.finalize(block, sems)
    return nc


_NC_CACHE = {}


def kernel(L0_attn_x, L0_attn_W_qkv, L0_attn_W_out, L0_attn_b_out,
           L0_ffn_w_gate, L0_ffn_w_up, L0_ffn_w_down):
    x = np.ascontiguousarray(np.asarray(L0_attn_x, dtype=np.float32))
    wqkv = np.ascontiguousarray(np.asarray(L0_attn_W_qkv, dtype=np.float32))
    wout = np.ascontiguousarray(np.asarray(L0_attn_W_out, dtype=np.float32))
    bout = np.asarray(L0_attn_b_out, dtype=np.float32)
    bfull = np.ascontiguousarray(np.broadcast_to(bout[None, :], (128, D)))
    wg = np.ascontiguousarray(np.asarray(L0_ffn_w_gate, dtype=np.float32))
    wu = np.ascontiguousarray(np.asarray(L0_ffn_w_up, dtype=np.float32))
    wd = np.ascontiguousarray(np.asarray(L0_ffn_w_down, dtype=np.float32))

    if "nc" not in _NC_CACHE:
        _NC_CACHE["nc"] = build_program()
    nc = _NC_CACHE["nc"]
    in_maps = [{"x": x[i], "wqkv": wqkv, "wout": wout, "bfull": bfull,
                "wg": wg, "wu": wu, "wd": wd} for i in range(8)]
    res = run_bass_kernel_spmd(nc, in_maps, core_ids=list(range(8)))
    return np.stack([np.asarray(r["out"]) for r in res.results], axis=0)


# version 17
# speedup vs baseline: 1.1735x; 1.0874x over previous
import numpy as np
import concourse.bass as bass
import concourse.mybir as mybir
from concourse.bass_utils import run_bass_kernel_spmd

F32 = mybir.dt.float32
BF16 = mybir.dt.bfloat16
I32 = mybir.dt.int32
AF = mybir.ActivationFunctionType
ALU = mybir.AluOpType
AX = mybir.AxisListType

S_LEN = 4096
D = 1024
I_FF = 2816
NKC = D // 128
NIC = I_FF // 128
T1 = 512
NT1 = S_LEN // T1
NB1 = T1 // 128
T2 = 128
NT2 = S_LEN // T2

ENGINES = ("sync", "gpsimd", "vector", "scalar", "tensor")


class Ev:
    __slots__ = ("eng", "emit", "deps", "hard", "dma", "needs_inc", "inc_val")

    def __init__(self, eng, emit, deps, hard, dma):
        self.eng = eng
        self.emit = emit
        self.deps = deps
        self.hard = hard
        self.dma = dma
        self.needs_inc = dma
        self.inc_val = None


def _flat(deps):
    out = []
    for d in deps:
        if d is None:
            continue
        if isinstance(d, (list, tuple)):
            out.extend(_flat(d))
        else:
            out.append(d)
    return out


class Prog:
    def __init__(self):
        self.events = {e: [] for e in ENGINES}

    def op(self, eng, emit, deps=(), dma=False, hard=()):
        ev = Ev(eng, emit, _flat(deps), _flat(hard), dma)
        self.events[eng].append(ev)
        return ev

    def finalize(self, block, sems):
        for eng in ENGINES:
            for ev in self.events[eng]:
                for d in ev.deps:
                    if d.eng != ev.eng or d.dma:
                        d.needs_inc = True
                for d in ev.hard:
                    d.needs_inc = True
        for eng in ENGINES:
            c, dc = 0, 0
            for ev in self.events[eng]:
                if ev.dma:
                    dc += 1
                    ev.inc_val = 16 * dc
                elif ev.needs_inc:
                    c += 1
                    ev.inc_val = c

        def emit_stream(eng_obj, eng_name):
            hw = {}
            for ev in self.events[eng_name]:
                for d in ev.deps + ev.hard:
                    if d not in ev.hard and d.eng == ev.eng and (
                            not d.dma or ev.dma):
                        continue
                    sname = ("d_" if d.dma else "c_") + d.eng
                    v = d.inc_val
                    if hw.get(sname, -1) >= v:
                        continue
                    eng_obj.wait_ge(sems[sname], v)
                    hw[sname] = v
                inst = ev.emit(eng_obj)
                if ev.needs_inc:
                    assert inst is not None, f"{eng_name} event needs inc"
                    sname = ("d_" if ev.dma else "c_") + ev.eng
                    inst.then_inc(sems[sname], 16 if ev.dma else 1)

        for name in ENGINES:
            getattr(block, name)(lambda eng, n=name: emit_stream(eng, n))


def emit_quant(p, deps, x_ap, F, sg, qa, qb, qc, bmax, ssc, rcp, r_ap, out_ap,
               hard0=()):
    nb = F // 16
    b3 = x_ap.rearrange("p (s n) -> p s n", n=16)
    e_bm = p.op("vector", lambda v: v.tensor_reduce(
        bmax[:, :nb], b3, axis=AX.X, op=ALU.max, apply_absolute_value=True),
        deps=deps, hard=hard0)
    e_s = p.op("vector", lambda v: v.tensor_scalar(
        ssc[:, :nb], bmax[:, :nb], 1.0 / 12.0, 5e-13, ALU.mult, ALU.max),
        hard=[e_bm])
    e_s4 = p.op("vector", lambda v: v.tensor_scalar(
        rcp[:, :nb], bmax[:, :nb], 1.0 / 24.0, 2.5e-13, ALU.mult, ALU.max),
        hard=[e_bm])
    e_rc = p.op("vector", lambda v: v.reciprocal(rcp[:, :nb], rcp[:, :nb]),
                hard=[e_s4])
    rb = rcp[:, :nb].unsqueeze(-1).broadcast_to((128, nb, 16))
    e_r = p.op("vector", lambda v: v.tensor_tensor(
        r_ap.rearrange("p (s n) -> p s n", n=16), x_ap.rearrange(
            "p (s n) -> p s n", n=16), rb, ALU.mult), deps=deps, hard=[e_rc])
    e_sg = p.op("scalar", lambda s: s.activation(sg[:, :F], r_ap, AF.Sign),
                deps=[e_r])
    e_u = p.op("scalar", lambda s: s.activation(r_ap, r_ap, AF.Abs),
               deps=[e_sg])
    u = r_ap
    e_t = p.op("vector", lambda v: v.tensor_scalar(
        qa[:, :F], u, 1.0, None, ALU.is_gt), deps=[e_u])
    for m, dst, src in ((3.0, qb, qa), (5.0, qa, qb), (7.0, qb, qa)):
        e_t = p.op("vector", lambda v, m=m, dst=dst, src=src:
                   v.scalar_tensor_tensor(dst[:, :F], u, m, src[:, :F],
                                          ALU.is_gt, ALU.add), deps=[e_t])
    e_last = e_t
    for m, d in ((10.0, 2.0), (14.0, 2.0), (20.0, 4.0)):
        e_v = p.op("vector", lambda v, m=m, d=d: v.tensor_scalar(
            qc[:, :F], u, m, d, ALU.is_gt, ALU.mult), deps=[e_last])
        e_last = p.op("vector", lambda v: v.tensor_tensor(
            qb[:, :F], qb[:, :F], qc[:, :F], ALU.add), deps=[e_v])
    e_vlast = e_last
    e_z = p.op("vector", lambda v: v.tensor_tensor(
        qb[:, :F], qb[:, :F], sg[:, :F], ALU.mult), deps=[e_last, e_sg])
    sb = ssc[:, :nb].unsqueeze(-1).broadcast_to((128, nb, 16))
    e_q = p.op("vector", lambda v: v.tensor_tensor(
        out_ap.rearrange("p (s n) -> p s n", n=16),
        qb[:, :F].rearrange("p (s n) -> p s n", n=16), sb,
        ALU.mult), deps=[e_z])
    return {"q": e_q, "r": e_r, "u": e_u, "v_last": e_vlast, "sg": e_sg,
            "z": e_z}


def build_program(dbg=False):
    scratch_kind = "ExternalOutput" if dbg else "Internal"
    nc = bass.Bass("TRN2", target_bir_lowering=False, debug=False)
    x_d = nc.dram_tensor("x", [S_LEN, D], F32, kind="ExternalInput").ap()
    wqkv_d = nc.dram_tensor("wqkv", [D, 3 * D], F32, kind="ExternalInput").ap()
    wout_d = nc.dram_tensor("wout", [D, D], F32, kind="ExternalInput").ap()
    bfull_d = nc.dram_tensor("bfull", [128, D], F32, kind="ExternalInput").ap()
    wg_d = nc.dram_tensor("wg", [I_FF, D], F32, kind="ExternalInput").ap()
    wu_d = nc.dram_tensor("wu", [I_FF, D], F32, kind="ExternalInput").ap()
    wd_d = nc.dram_tensor("wd", [D, I_FF], F32, kind="ExternalInput").ap()
    out_d = nc.dram_tensor("out", [S_LEN, D], F32, kind="ExternalOutput").ap()
    x2s_d = nc.dram_tensor("x2s", [S_LEN, D], F32, kind=scratch_kind).ap()
    wgq_d = nc.dram_tensor("wgq_s", [128, NKC * I_FF], BF16, kind=scratch_kind).ap()
    wuq_d = nc.dram_tensor("wuq_s", [128, NKC * I_FF], BF16, kind=scratch_kind).ap()
    wdq_d = nc.dram_tensor("wdq_s", [128, NIC * D], BF16, kind=scratch_kind).ap()

    p = Prog()
    dbg_tokens = {}

    def dump(name, ap, deps, shape, dt):
        if not dbg:
            return None
        dd = nc.dram_tensor("dbg_" + name, list(shape), dt,
                            kind="ExternalOutput").ap()
        ev = p.op("sync", lambda sy: sy.dma_start(out=dd[:, :], in_=ap),
                  deps=deps, dma=True)
        dbg_tokens[name] = ev
        return ev

    from contextlib import ExitStack

    with ExitStack() as st:
        sems = {}
        for e in ENGINES:
            sems["c_" + e] = st.enter_context(nc.semaphore("c_" + e))
            sems["d_" + e] = st.enter_context(nc.semaphore("d_" + e))

        iota_t = st.enter_context(nc.sbuf_tensor("iota_t", [128, 128], I32))
        ident = st.enter_context(nc.sbuf_tensor("ident", [128, 128], BF16))
        mask = st.enter_context(nc.sbuf_tensor("mask", [128, 128], F32))
        e_io = p.op("gpsimd", lambda g: g.iota(
            iota_t[:], pattern=[[1, 128]], base=0, channel_multiplier=-1))
        e_id = p.op("vector", lambda v: v.tensor_single_scalar(
            ident[:], iota_t[:], 0.0, ALU.is_equal), deps=[e_io])
        e_mk = p.op("vector", lambda v: v.tensor_scalar(
            mask[:], iota_t[:], 0.0, -3.0e38, ALU.is_gt, ALU.mult),
            deps=[e_io])

        block = st.enter_context(nc.Block())

        with ExitStack() as st0:
            wch = [st0.enter_context(nc.sbuf_tensor(f"wch{i}", [128, I_FF], F32))
                   for i in range(2)]
            r_s = st0.enter_context(nc.sbuf_tensor("r0", [128, I_FF], F32))
            sg_s = st0.enter_context(nc.sbuf_tensor("sg0", [128, I_FF], BF16))
            qa = st0.enter_context(nc.sbuf_tensor("qa0", [128, I_FF], BF16))
            qb = st0.enter_context(nc.sbuf_tensor("qb0", [128, I_FF], BF16))
            qc = st0.enter_context(nc.sbuf_tensor("qc0", [128, I_FF], BF16))
            bmax = st0.enter_context(nc.sbuf_tensor("bmax0", [128, I_FF // 16], F32))
            ssc = st0.enter_context(nc.sbuf_tensor("ssc0", [128, I_FF // 16], F32))
            rcp = st0.enter_context(nc.sbuf_tensor("rcp0", [128, I_FF // 16], F32))
            wqb = st0.enter_context(nc.sbuf_tensor("wqb0", [128, I_FF], BF16))
            strips = st0.enter_context(nc.sbuf_tensor(
                "strips0", [128, NKC * I_FF], BF16))
            pst = st0.enter_context(nc.psum_tensor("pst0", [128, 1024], BF16))

            st_prev = None

            def do_weight(w_dram, n_chunks, F, n_tp, scratch_d, st_prev_dep):
                ld_reuse = [None, None]
                q_prev = None
                tp_prev = st_prev_dep
                last_ev = None
                for c in range(n_chunks):
                    buf = wch[c % 2]
                    e_ld = p.op("sync", lambda sy, b=buf, c=c: sy.dma_start(
                        out=b[:, :F], in_=w_dram[c * 128:(c + 1) * 128, :]),
                        deps=[ld_reuse[c % 2]], dma=True)
                    qt = emit_quant(p, [e_ld, q_prev], buf[:, :F], F,
                                    sg_s, qa, qb, qc, bmax, ssc, rcp,
                                    r_s[:, :F], wqb[:, :F])
                    ld_reuse[c % 2] = qt["r"]
                    ev_list = []
                    for b0 in range(0, n_tp, 8):
                        bn = min(8, n_tp - b0)
                        tps = []
                        for j in range(bn):
                            e_tp = p.op("tensor", lambda t, j=j, b0=b0: t.transpose(
                                pst[:, j * 128:(j + 1) * 128],
                                wqb[:, (b0 + j) * 128:(b0 + j + 1) * 128],
                                ident[:]), deps=[qt["q"], e_id, tp_prev])
                            tps.append(e_tp)
                        e_ev = p.op("scalar", lambda s, b0=b0, bn=bn, c=c,
                                    n_chunks=n_chunks: s.copy(
                            strips[:].rearrange("p (t f) -> p t f",
                                                f=n_chunks * 128)
                            [:, b0:b0 + bn, c * 128:(c + 1) * 128],
                            pst[:, :bn * 128].rearrange("p (t f) -> p t f",
                                                        f=128)), deps=tps)
                        tp_prev = e_ev
                        ev_list.append(e_ev)
                        q_prev = tps[-1]
                    last_ev = ev_list[-1]
                e_st = p.op("sync", lambda sy: sy.dma_start(
                    out=scratch_d[:, :n_tp * n_chunks * 128],
                    in_=strips[:, :n_tp * n_chunks * 128]),
                    deps=[last_ev], dma=True)
                return e_st, q_prev, last_ev

            e_g, _, _ = do_weight(wg_d, NIC, D, NKC, wgq_d, None)
            e_u_, _, _ = do_weight(wu_d, NIC, D, NKC, wuq_d, e_g)
            e_dw, p0_lastT, p0_lastS = do_weight(wd_d, NKC, I_FF, NIC, wdq_d,
                                                 e_u_)
            p0_dma = [e_g, e_u_, e_dw]

        with ExitStack() as st1:
            wqkv_sb = st1.enter_context(
                nc.sbuf_tensor("wqkv_sb", [128, NKC * 3 * D], BF16))
            woutT = st1.enter_context(nc.sbuf_tensor("woutT", [128, NKC * D], BF16))
            bfull = st1.enter_context(nc.sbuf_tensor("bfull_sb", [128, D], F32))
            with ExitStack() as st1a:
                woutb = st1a.enter_context(
                    nc.sbuf_tensor("woutb", [128, NKC * D], BF16))
                pstw = st1a.enter_context(nc.psum_tensor("pstw", [128, 1024], BF16))
                e_wq_ld = []
                for kc in range(NKC):
                    e1 = p.op("gpsimd", lambda g, kc=kc: g.dma_start(
                        out=wqkv_sb[:, kc * 3 * D:(kc + 1) * 3 * D],
                        in_=wqkv_d[kc * 128:(kc + 1) * 128, :]),
                        deps=p0_dma, dma=True)
                    e2 = p.op("gpsimd", lambda g, kc=kc: g.dma_start(
                        out=woutb[:, kc * D:(kc + 1) * D],
                        in_=wout_d[kc * 128:(kc + 1) * 128, :]),
                        deps=p0_dma, dma=True)
                    e_wq_ld += [e1, e2]
                e_bf_ld = p.op("sync", lambda sy: sy.dma_start(
                    out=bfull[:], in_=bfull_d[:, :]), deps=p0_dma, dma=True)
                prev_ev = p0_lastS
                e_wt_ev = []
                for jc in range(NKC):
                    tps = []
                    for ic in range(NKC):
                        e_tp = p.op("tensor", lambda t, ic=ic, jc=jc: t.transpose(
                            pstw[:, ic * 128:(ic + 1) * 128],
                            woutb[:, ic * D + jc * 128: ic * D + jc * 128 + 128],
                            ident[:]),
                            deps=e_wq_ld + [e_id, prev_ev, p0_lastT]
                            if ic == 0 and jc == 0 else
                            (e_wq_ld + [prev_ev] if ic == 0 else []))
                        tps.append(e_tp)
                    e_ev = p.op("scalar", lambda s, jc=jc: s.copy(
                        woutT[:, jc * D:(jc + 1) * D], pstw[:]), deps=tps)
                    prev_ev = e_ev
                    e_wt_ev.append(e_ev)

            xf = [st1.enter_context(nc.sbuf_tensor(f"xf{i}", [128, NB1 * D], F32))
                  for i in range(2)]
            xbf = st1.enter_context(nc.sbuf_tensor("xbf", [128, NB1 * D], BF16))
            xT = st1.enter_context(nc.sbuf_tensor("xT", [128, NKC * T1], BF16))
            q_sb = st1.enter_context(nc.sbuf_tensor("q_sb", [128, NKC * T1], BF16))
            k_sb = st1.enter_context(nc.sbuf_tensor("k_sb", [128, NKC * T1], BF16))
            v_sb = st1.enter_context(nc.sbuf_tensor("v_sb", [128, NB1 * D], BF16))
            ao_sb = st1.enter_context(nc.sbuf_tensor("ao_sb", [128, NKC * T1], BF16))
            xb_sb = st1.enter_context(nc.sbuf_tensor("xb_sb", [128, NB1 * D], F32))
            y_sb = st1.enter_context(nc.sbuf_tensor("y_sb", [128, NB1 * D], F32))
            sm_sb = [st1.enter_context(nc.sbuf_tensor(f"sm_sb{i}", [128, 128], F32)) for i in range(2)]
            pn_sb = [st1.enter_context(nc.sbuf_tensor(f"pn_sb{i}", [128, 128], BF16)) for i in range(4)]
            pT_sb = [st1.enter_context(nc.sbuf_tensor(f"pT_sb{i}", [128, 128], BF16)) for i in range(2)]
            nmax = [st1.enter_context(nc.sbuf_tensor(f"nmax{i}", [128, 1], F32)) for i in range(2)]
            rsum = [st1.enter_context(nc.sbuf_tensor(f"rsum{i}", [128, 1], F32)) for i in range(2)]
            rred = st1.enter_context(nc.sbuf_tensor("rred", [128, NB1], F32))
            negm = st1.enter_context(nc.sbuf_tensor("negm", [128, NB1], F32))
            ssq = st1.enter_context(nc.sbuf_tensor("ssq", [128, NB1], F32))
            rstd = st1.enter_context(nc.sbuf_tensor("rstd", [128, NB1], F32))
            psA = [st1.enter_context(nc.psum_tensor(f"psA{i}", [128, 512], F32))
                   for i in range(4)]
            psT = st1.enter_context(nc.psum_tensor("psT1", [128, 1024], BF16))

            wq_dep = e_wq_ld + [e_bf_ld, e_mk]
            tl = {"xf_free": [None, None], "y_store": None, "xbf_free": None,
                  "psA_ev": [None] * 4, "psT_ev": e_wt_ev[-1], "ln_s": None,
                  "sm_prev": [None, None], "pn_prev": [None] * 4}
            p1_lastV = p1_lastS = p1_lastT = None
            p1_dma = None

            dbg_att = []
            for tt in range(NT1):
                pb = tt % 2
                e_lds = []
                for b in range(NB1):
                    e = p.op("sync", lambda sy, b=b, tt=tt, pb=pb: sy.dma_start(
                        out=xf[pb][:, b * D:(b + 1) * D],
                        in_=x_d[tt * T1 + b * 128: tt * T1 + (b + 1) * 128, :]),
                        deps=[tl["xf_free"][pb]] +
                             ([e_wt_ev[-1]] if tt < 2 else []), dma=True)
                    e_lds.append(e)
                e_cast = p.op("scalar", lambda s, pb=pb: s.copy(
                    xbf[:], xf[pb][:]), deps=e_lds + [tl["xbf_free"]])
                e_xb = p.op("vector", lambda v, pb=pb: v.tensor_tensor(
                    xb_sb[:].rearrange("p (b f) -> p b f", f=D),
                    xf[pb][:].rearrange("p (b f) -> p b f", f=D),
                    bfull[:].unsqueeze(1).broadcast_to((128, NB1, D)), ALU.add),
                    deps=e_lds + [e_bf_ld, tl["ln_s"]])

                e_xT_ev = []
                dep_psT = tl["psT_ev"]
                for b in range(NB1):
                    tps = []
                    for kc in range(NKC):
                        e_tp = p.op("tensor", lambda t, b=b, kc=kc: t.transpose(
                            psT[:, kc * 128:(kc + 1) * 128],
                            xbf[:, b * D + kc * 128: b * D + kc * 128 + 128],
                            ident[:]), deps=[e_cast, e_id, dep_psT]
                            if kc == 0 else [])
                        tps.append(e_tp)
                    e_ev = p.op("scalar", lambda s, b=b: s.copy(
                        xT[:].rearrange("p (kc t) -> p kc t", t=T1)
                        [:, :, b * 128:(b + 1) * 128],
                        psT[:].rearrange("p (kc f) -> p kc f", f=128)),
                        deps=tps)
                    dep_psT = e_ev
                    e_xT_ev.append(e_ev)
                tl["psT_ev"] = e_xT_ev[-1]

                e_qk_ev = []
                for oc in range(16):
                    ps = psA[oc % 2]
                    mms = []
                    dep0 = e_xT_ev + wq_dep + [tl["psA_ev"][oc % 2]]
                    for kc in range(NKC):
                        e_mm = p.op("tensor", lambda t, oc=oc, kc=kc, ps=ps:
                                    t.matmul(
                            ps[:],
                            wqkv_sb[:, kc * 3 * D + oc * 128:
                                    kc * 3 * D + oc * 128 + 128],
                            xT[:, kc * T1:(kc + 1) * T1],
                            start=(kc == 0), stop=(kc == NKC - 1)),
                            deps=dep0 if kc == 0 else [])
                        mms.append(e_mm)
                    dst = q_sb if oc < 8 else k_sb
                    so = oc % 8
                    e_ev = p.op("scalar", lambda s, dst=dst, so=so, ps=ps:
                                s.copy(dst[:, so * T1:(so + 1) * T1], ps[:]),
                                deps=[mms[-1]])
                    tl["psA_ev"][oc % 2] = e_ev
                    e_qk_ev.append(e_ev)

                e_v_ev = []
                for b in range(NB1):
                    for half in range(2):
                        i = 2 + half
                        ps = psA[i]
                        mms = []
                        dep0 = e_xT_ev + wq_dep + [tl["psA_ev"][i]]
                        for kc in range(NKC):
                            e_mm = p.op("tensor", lambda t, b=b, half=half,
                                        kc=kc, ps=ps: t.matmul(
                                ps[:],
                                xT[:, kc * T1 + b * 128: kc * T1 + b * 128 + 128],
                                wqkv_sb[:, kc * 3 * D + 2 * D + half * 512:
                                        kc * 3 * D + 2 * D + half * 512 + 512],
                                start=(kc == 0), stop=(kc == NKC - 1)),
                                deps=dep0 if kc == 0 else [])
                            mms.append(e_mm)
                        e_ev = p.op("scalar", lambda s, b=b, half=half, ps=ps:
                                    s.copy(
                            v_sb[:, b * D + half * 512: b * D + half * 512 + 512],
                            ps[:]), deps=[mms[-1]])
                        tl["psA_ev"][i] = e_ev
                        e_v_ev.append(e_ev)

                e_att_ev = [None] * NB1
                sc_done = []
                for b in range(NB1):
                    psc = psA[b]
                    mms = []
                    dep0 = e_qk_ev + [tl["psA_ev"][b]] + (dbg_att if b == 0 else [])
                    for kc in range(NKC):
                        e_mm = p.op("tensor", lambda t, b=b, kc=kc, psc=psc:
                                    t.matmul(
                            psc[:, 0:128],
                            q_sb[:, kc * T1 + b * 128: kc * T1 + b * 128 + 128],
                            k_sb[:, kc * T1 + b * 128: kc * T1 + b * 128 + 128],
                            start=(kc == 0), stop=(kc == NKC - 1)),
                            deps=dep0 if kc == 0 else [])
                        mms.append(e_mm)
                    sc_done.append(mms[-1])
                dbg_att = []
                sm_prev = [tl["sm_prev"][0], tl["sm_prev"][1]]
                pn_prev = list(tl["pn_prev"])
                e_sm_l, e_pn_l = [], []
                for b in range(NB1):
                    i = b % 2
                    e_sm = p.op("vector", lambda v, b=b, i=i:
                                v.scalar_tensor_tensor(
                        sm_sb[i][:], psA[b][:, 0:128], 1.0 / 32.0, mask[:],
                        ALU.mult, ALU.add),
                        deps=[sc_done[b], e_mk, sm_prev[i]])
                    sm_prev[i] = e_sm
                    tl["psA_ev"][b] = e_sm
                    e_sm_l.append(e_sm)
                    e_nm = p.op("vector", lambda v, i=i: v.reduce_max(
                        nmax[i][:], sm_sb[i][:], axis=AX.X, negate=True),
                        hard=[e_sm])
                    e_p = p.op("scalar", lambda s, b=b, i=i: s.activation(
                        pn_sb[b][:], sm_sb[i][:], AF.Exp, bias=nmax[i][:, 0:1],
                        accum_out=rsum[i][:]), deps=[e_nm, pn_prev[b]])
                    e_rr = p.op("vector", lambda v, i=i: v.tensor_scalar(
                        rsum[i][:], rsum[i][:], 1e-6, None, ALU.add),
                        deps=[e_p])
                    e_rc2 = p.op("vector", lambda v, i=i: v.reciprocal(
                        rsum[i][:], rsum[i][:]), hard=[e_rr])
                    e_pn = p.op("scalar", lambda sc, b=b, i=i: sc.activation(
                        pn_sb[b][:], pn_sb[b][:], AF.Identity,
                        scale=rsum[i][:, 0:1]), deps=[e_rc2])
                    e_pn_l.append(e_pn)
                pair_free = {0: [e_sm_l[0], e_sm_l[1]],
                             1: [e_sm_l[2], e_sm_l[3]]}
                for b in range(NB1):
                    i = b % 2
                    e_ptp = p.op("tensor", lambda t, b=b: t.transpose(
                        psT[:, b * 128:(b + 1) * 128], pn_sb[b][:], ident[:]),
                        deps=[e_pn_l[b], tl["psT_ev"]])
                    e_pte = p.op("scalar", lambda s, b=b, i=i: s.copy(
                        pT_sb[i][:], psT[:, b * 128:(b + 1) * 128]),
                        deps=[e_ptp])
                    tl["psT_ev"] = e_pte
                    pn_prev[b] = e_ptp
                    pg = b % 2
                    att_mms = []
                    for dc in range(NKC):
                        ps = psA[2 * pg + dc // 4]
                        dep = [e_pte]
                        if dc == 0:
                            dep += e_v_ev + pair_free[pg]
                        e_mm = p.op("tensor", lambda t, b=b, dc=dc, ps=ps,
                                    i=i: t.matmul(
                            ps[:, (dc % 4) * 128:(dc % 4) * 128 + 128],
                            v_sb[:, b * D + dc * 128: b * D + dc * 128 + 128],
                            pT_sb[i][:], start=True, stop=True), deps=dep)
                        att_mms.append(e_mm)
                    e_ae = []
                    for k2 in range(2):
                        e_ev = p.op("scalar", lambda s, b=b, k2=k2, pg=pg: s.copy(
                            ao_sb[:].rearrange("p (kc t) -> p kc t", t=T1)
                            [:, 4 * k2:4 * k2 + 4, b * 128:(b + 1) * 128],
                            psA[2 * pg + k2][:].rearrange(
                                "p (c f) -> p c f", f=128)),
                            deps=att_mms)
                        tl["psA_ev"][2 * pg + k2] = e_ev
                        e_ae.append(e_ev)
                    pair_free[pg] = e_ae
                    e_att_ev[b] = e_ae
                tl["sm_prev"] = sm_prev
                tl["pn_prev"] = pn_prev
                e_y = []
                for b in range(NB1):
                    for half in range(2):
                        i = half
                        ps = psA[i]
                        mms = []
                        dep0 = e_att_ev[b] + [tl["psA_ev"][i]] + e_wt_ev
                        for jc in range(NKC):
                            e_mm = p.op("tensor", lambda t, b=b, half=half,
                                        jc=jc, ps=ps: t.matmul(
                                ps[:],
                                ao_sb[:, jc * T1 + b * 128:
                                      jc * T1 + b * 128 + 128],
                                woutT[:, jc * D + half * 512:
                                      jc * D + half * 512 + 512],
                                start=(jc == 0), stop=(jc == NKC - 1)),
                                deps=dep0 if jc == 0 else [])
                            mms.append(e_mm)
                        e_yv = p.op("vector", lambda v, b=b, half=half, ps=ps:
                                    v.tensor_tensor(
                            y_sb[:, b * D + half * 512:
                                 b * D + half * 512 + 512],
                            ps[:],
                            xb_sb[:, b * D + half * 512:
                                  b * D + half * 512 + 512],
                            ALU.add), deps=[mms[-1], e_xb, tl["y_store"]])
                        tl["psA_ev"][i] = e_yv
                        e_y.append(e_yv)
                        p1_lastT = mms[-1]

                if dbg and tt == 0:
                    dump("xT", xT[:], e_xT_ev, (128, NKC * T1), BF16)
                    dump("q", q_sb[:], e_qk_ev, (128, NKC * T1), BF16)
                    dump("k", k_sb[:], e_qk_ev, (128, NKC * T1), BF16)
                    dump("v", v_sb[:], e_v_ev, (128, NB1 * D), BF16)
                    dump("ao", ao_sb[:], _flat(e_att_ev), (128, NKC * T1), BF16)
                    dump("y", y_sb[:], e_y, (128, NB1 * D), F32)
                    dump("xb", xb_sb[:], [e_xb], (128, NB1 * D), F32)
                e_red = p.op("vector", lambda v: v.reduce_sum(
                    rred[:], y_sb[:].rearrange("p (b f) -> p b f", f=D),
                    axis=AX.X), deps=e_y, hard=e_y[-2:])
                e_ng = p.op("vector", lambda v: v.tensor_scalar(
                    negm[:], rred[:], -1.0 / D, None, ALU.mult),
                    hard=[e_red])
                e_yc, e_sq = [], []
                dbg_dep = [dbg_tokens.get("xb"), dbg_tokens.get("y")] if (
                    dbg and tt == 0) else []
                for b in range(NB1):
                    e1 = p.op("scalar", lambda s, b=b: s.activation(
                        xb_sb[:, b * D:(b + 1) * D],
                        y_sb[:, b * D:(b + 1) * D],
                        AF.Identity, bias=negm[:, b:b + 1]),
                        deps=[e_ng] + dbg_dep)
                    e_yc.append(e1)
                for b in range(NB1):
                    e1 = p.op("scalar", lambda s, b=b: s.activation(
                        y_sb[:, b * D:(b + 1) * D],
                        xb_sb[:, b * D:(b + 1) * D],
                        AF.Square, accum_out=ssq[:, b:b + 1]),
                        deps=[e_yc[b]])
                    e_sq.append(e1)
                e_var = p.op("vector", lambda v: v.tensor_scalar(
                    negm[:], ssq[:], 1.0 / D, 1e-5, ALU.mult, ALU.add),
                    deps=e_sq)
                e_sd = p.op("scalar", lambda s: s.activation(
                    ssq[:], negm[:], AF.Ln), deps=[e_var])
                e_rs = p.op("scalar", lambda s: s.activation(
                    rstd[:], ssq[:], AF.Exp, scale=-0.5), deps=[e_sd])
                e_x2 = []
                for b in range(NB1):
                    e1 = p.op("scalar", lambda sc, b=b: sc.activation(
                        y_sb[:, b * D:(b + 1) * D],
                        xb_sb[:, b * D:(b + 1) * D],
                        AF.Identity, scale=rstd[:, b:b + 1]), hard=[e_rs])
                    e_x2.append(e1)
                tl["ln_s"] = e_sq[-1]
                e_sts = []
                for b in range(NB1):
                    e1 = p.op("sync", lambda sy, b=b, tt=tt: sy.dma_start(
                        out=x2s_d[tt * T1 + b * 128: tt * T1 + (b + 1) * 128, :],
                        in_=y_sb[:, b * D:(b + 1) * D]),
                        deps=[e_x2[b]], dma=True)
                    e_sts.append(e1)
                tl["y_store"] = e_sts[-1]
                tl["xf_free"][pb] = [e_xb, e_cast]
                tl["xbf_free"] = tps[-1] if False else e_xT_ev
                tl["xbf_free"] = e_xT_ev[-1]
                p1_lastV = e_rs
                p1_lastS = e_x2[-1]
                p1_dma = e_sts[-1]

            p1_bar = [p1_lastV, p1_lastS, p1_lastT]

        with ExitStack() as st2:
            wgqT = st2.enter_context(nc.sbuf_tensor("wgqT", [128, NKC * I_FF], BF16))
            wuqT = st2.enter_context(nc.sbuf_tensor("wuqT", [128, NKC * I_FF], BF16))
            wdqT = st2.enter_context(nc.sbuf_tensor("wdqT", [128, NIC * D], BF16))
            x2t = [st2.enter_context(nc.sbuf_tensor(f"x2t{i}", [128, D], F32))
                   for i in range(2)]
            r2 = st2.enter_context(nc.sbuf_tensor("r2", [128, I_FF], F32))
            sg2 = st2.enter_context(nc.sbuf_tensor("sg2", [128, I_FF], BF16))
            qa2 = st2.enter_context(nc.sbuf_tensor("qa2", [128, I_FF], BF16))
            qb2 = st2.enter_context(nc.sbuf_tensor("qb2", [128, I_FF], BF16))
            qc2 = st2.enter_context(nc.sbuf_tensor("qc2", [128, I_FF], BF16))
            bmax2 = st2.enter_context(nc.sbuf_tensor("bmax2", [128, I_FF // 16], F32))
            ssc2 = st2.enter_context(nc.sbuf_tensor("ssc2", [128, I_FF // 16], F32))
            rcp2 = st2.enter_context(nc.sbuf_tensor("rcp2", [128, I_FF // 16], F32))
            rx = st2.enter_context(nc.sbuf_tensor("rx", [128, D], F32))
            sgx = st2.enter_context(nc.sbuf_tensor("sgx", [128, D], BF16))
            qxa = st2.enter_context(nc.sbuf_tensor("qxa", [128, D], BF16))
            qxb = st2.enter_context(nc.sbuf_tensor("qxb", [128, D], BF16))
            qxc = st2.enter_context(nc.sbuf_tensor("qxc", [128, D], BF16))
            bmx = st2.enter_context(nc.sbuf_tensor("bmx", [128, D // 16], F32))
            ssx = st2.enter_context(nc.sbuf_tensor("ssx", [128, D // 16], F32))
            rcx = st2.enter_context(nc.sbuf_tensor("rcx", [128, D // 16], F32))
            xqT = [st2.enter_context(nc.sbuf_tensor(f"xqT{i}", [128, NKC * 128],
                                                    BF16)) for i in range(2)]
            hqT = st2.enter_context(nc.sbuf_tensor("hqT", [128, NIC * 128], BF16))
            y2 = st2.enter_context(nc.sbuf_tensor("y2", [128, D], F32))
            o_sb = st2.enter_context(nc.sbuf_tensor("o_sb", [128, D], F32))
            nm2 = st2.enter_context(nc.sbuf_tensor("nm2", [128, 1], F32))
            ss2 = st2.enter_context(nc.sbuf_tensor("ss2", [128, 1], F32))
            rst2 = st2.enter_context(nc.sbuf_tensor("rst2", [128, 1], F32))
            psg = [st2.enter_context(nc.psum_tensor(f"psg{i}", [128, 512], F32))
                   for i in range(2)]
            psu = [st2.enter_context(nc.psum_tensor(f"psu{i}", [128, 512], F32))
                   for i in range(2)]
            psd = [st2.enter_context(nc.psum_tensor(f"psd{i}", [128, 512], F32))
                   for i in range(2)]
            psT2 = st2.enter_context(nc.psum_tensor("psT2", [128, 1024], BF16))

            e_wld = []
            for src, dst in ((wgq_d, wgqT), (wuq_d, wuqT), (wdq_d, wdqT)):
                e1 = p.op("sync", lambda sy, src=src, dst=dst: sy.dma_start(
                    out=dst[:], in_=src[:, :]),
                    deps=p0_dma + [p1_dma] + p1_bar, dma=True)
                e_wld.append(e1)

            t2 = {"x2_free": [None, None], "o_store": None,
                  "psT_ev": tl["psT_ev"],
                  "psg_ev": [tl["psA_ev"][0], tl["psA_ev"][1]],
                  "psu_ev": [tl["psA_ev"][2], tl["psA_ev"][3]],
                  "psd_ev": [None, None], "r_free": None,
                  "h_free": None, "y2_s": None,
                  "xq_tp_any": None,
                  "xqT_ev": [None, None],
                  "gu_last": [None, None]}

            nblocks = [(0, 512), (512, 512), (1024, 512), (1536, 512),
                       (2048, 512), (2560, 256)]

            def emit_xquant_v(tt):
                pb = tt % 2
                e_ld = p.op("sync", lambda sy, tt=tt, pb=pb: sy.dma_start(
                    out=x2t[pb][:], in_=x2s_d[tt * T2:(tt + 1) * T2, :]),
                    deps=[t2["x2_free"][pb]] + e_wld, dma=True)
                return emit_quant(
                    p, [e_ld, t2["xq_tp_any"], t2["r_free"]],
                    x2t[pb][:], D, sgx, qxa, qxb, qxc, bmx, ssx, rcx,
                    rx[:], qxa[:])

            def emit_xquant_t(tt, qx):
                pb = tt % 2
                tps = []
                for kc in range(NKC):
                    e_tp = p.op("tensor", lambda t, kc=kc: t.transpose(
                        psT2[:, kc * 128:(kc + 1) * 128],
                        qxa[:, kc * 128:(kc + 1) * 128], ident[:]),
                        deps=[qx["q"], e_id, t2["psT_ev"]] if kc == 0 else [])
                    tps.append(e_tp)
                e_xqe = p.op("scalar", lambda s, pb=pb: s.copy(
                    xqT[pb][:], psT2[:]), deps=tps + [t2["gu_last"][pb]])
                t2["psT_ev"] = e_xqe
                t2["xq_tp_any"] = tps[-1]
                t2["xqT_ev"][pb] = e_xqe

            qx = emit_xquant_v(0)
            emit_xquant_t(0, qx)

            for tt in range(NT2):
                pb = tt % 2
                e_h = []
                for ps_i in range(3):
                    nb0, nb1 = nblocks[2 * ps_i], nblocks[2 * ps_i + 1]
                    g_mms, u_mms = [[], []], [[], []]
                    dep0 = [t2["xqT_ev"][pb]] + e_wld + t2["psg_ev"] + t2["psu_ev"]
                    first = True
                    for kc in range(NKC):
                        for j, (off, w) in enumerate((nb0, nb1)):
                            e1 = p.op("tensor", lambda t, kc=kc, off=off, w=w,
                                      j=j, pb=pb: t.matmul(
                                psg[j][:, :w],
                                xqT[pb][:, kc * 128:(kc + 1) * 128],
                                wgqT[:, kc * I_FF + off: kc * I_FF + off + w],
                                start=(kc == 0), stop=(kc == NKC - 1)),
                                deps=dep0 if first else [])
                            first = False
                            g_mms[j].append(e1)
                            e2 = p.op("tensor", lambda t, kc=kc, off=off, w=w,
                                      j=j, pb=pb: t.matmul(
                                psu[j][:, :w],
                                xqT[pb][:, kc * 128:(kc + 1) * 128],
                                wuqT[:, kc * I_FF + off: kc * I_FF + off + w],
                                start=(kc == 0), stop=(kc == NKC - 1)),
                                deps=[])
                            u_mms[j].append(e2)
                    for j, (off, w) in enumerate((nb0, nb1)):
                        e_si = p.op("scalar", lambda s, off=off, w=w, j=j:
                                    s.activation(
                            r2[:, off:off + w], psg[j][:, :w], AF.Silu),
                            deps=[g_mms[j][-1], u_mms[j][-1], t2["h_free"]])
                        e_hm = p.op("vector", lambda v, off=off, w=w, j=j:
                                    v.tensor_tensor(
                            r2[:, off:off + w], r2[:, off:off + w],
                            psu[j][:, :w], ALU.mult), deps=[e_si])
                        t2["psg_ev"][j] = e_hm
                        t2["psu_ev"][j] = e_hm
                        e_h.append(e_hm)
                t2["gu_last"][pb] = u_mms[1][-1]

                qh = emit_quant(p, e_h, r2[:], I_FF,
                                sg2, qa2, qb2, qc2, bmax2, ssc2, rcp2,
                                r2[:], qa2[:], hard0=e_h[-2:])
                t2["h_free"] = qh["v_last"]

                if tt + 1 < NT2:
                    qx = emit_xquant_v(tt + 1)

                e_hqe = []
                dep_psT = t2["psT_ev"]
                last_tp = None
                for b0 in range(0, NIC, 8):
                    bn = min(8, NIC - b0)
                    tps2 = []
                    for j in range(bn):
                        e_tp = p.op("tensor", lambda t, j=j, b0=b0: t.transpose(
                            psT2[:, j * 128:(j + 1) * 128],
                            qa2[:, (b0 + j) * 128:(b0 + j + 1) * 128],
                            ident[:]),
                            deps=[qh["q"], dep_psT] if j == 0 else [])
                        tps2.append(e_tp)
                    e_ev = p.op("scalar", lambda s, b0=b0, bn=bn: s.copy(
                        hqT[:, b0 * 128:(b0 + bn) * 128], psT2[:, :bn * 128]),
                        deps=tps2)
                    dep_psT = e_ev
                    e_hqe.append(e_ev)
                    last_tp = tps2[-1]
                t2["psT_ev"] = e_hqe[-1]

                d_mms = [[], []]
                dep0 = e_hqe + e_wld + t2["psd_ev"]
                first = True
                for nc22 in range(NIC):
                    for j in range(2):
                        e1 = p.op("tensor", lambda t, nc22=nc22, j=j: t.matmul(
                            psd[j][:],
                            hqT[:, nc22 * 128:(nc22 + 1) * 128],
                            wdqT[:, nc22 * D + j * 512:
                                 nc22 * D + j * 512 + 512],
                            start=(nc22 == 0), stop=(nc22 == NIC - 1)),
                            deps=dep0 if first else [])
                        first = False
                        d_mms[j].append(e1)

                if tt + 1 < NT2:
                    emit_xquant_t(tt + 1, qx)

                e_y2 = []
                for j in range(2):
                    e1 = p.op("vector", lambda v, j=j, pb=pb: v.tensor_tensor(
                        y2[:, j * 512:(j + 1) * 512], psd[j][:],
                        x2t[pb][:, j * 512:(j + 1) * 512], ALU.add),
                        deps=[d_mms[j][-1], t2["y2_s"]])
                    t2["psd_ev"][j] = e1
                    e_y2.append(e1)
                t2["x2_free"][pb] = e_y2
                e_red = p.op("vector", lambda v: v.reduce_sum(
                    nm2[:], y2[:], axis=AX.X), deps=e_y2, hard=e_y2[-2:])
                e_ng = p.op("vector", lambda v: v.tensor_scalar(
                    nm2[:], nm2[:], -1.0 / D, None, ALU.mult), hard=[e_red])
                e_yc = p.op("scalar", lambda s: s.activation(
                    rx[:], y2[:], AF.Identity, bias=nm2[:, 0:1]),
                    deps=[e_ng, t2["r_free"], qx["v_last"]])
                e_sq = p.op("scalar", lambda s: s.activation(
                    y2[:], rx[:], AF.Square, accum_out=ss2[:]),
                    deps=[e_yc])
                t2["y2_s"] = e_sq
                e_var = p.op("vector", lambda v: v.tensor_scalar(
                    nm2[:], ss2[:], 1.0 / D, 1e-5, ALU.mult, ALU.add),
                    deps=[e_sq])
                e_sd = p.op("scalar", lambda s: s.activation(
                    ss2[:], nm2[:], AF.Ln), deps=[e_var])
                e_rs = p.op("scalar", lambda s: s.activation(
                    rst2[:], ss2[:], AF.Exp, scale=-0.5), deps=[e_sd])
                e_x3 = p.op("scalar", lambda sc: sc.activation(
                    o_sb[:], rx[:], AF.Identity, scale=rst2[:, 0:1]),
                    deps=[t2["o_store"]], hard=[e_rs])
                t2["r_free"] = e_x3
                e_st = p.op("sync", lambda sy, tt=tt: sy.dma_start(
                    out=out_d[tt * T2:(tt + 1) * T2, :], in_=o_sb[:]),
                    deps=[e_x3], dma=True)
                t2["o_store"] = e_st

            p.op("vector", lambda v: v.memset(nm2[:], 0.0),
                 deps=[t2["o_store"]])

        p.finalize(block, sems)
    return nc


_NC_CACHE = {}


def kernel(L0_attn_x, L0_attn_W_qkv, L0_attn_W_out, L0_attn_b_out,
           L0_ffn_w_gate, L0_ffn_w_up, L0_ffn_w_down):
    x = np.ascontiguousarray(np.asarray(L0_attn_x, dtype=np.float32))
    wqkv = np.ascontiguousarray(np.asarray(L0_attn_W_qkv, dtype=np.float32))
    wout = np.ascontiguousarray(np.asarray(L0_attn_W_out, dtype=np.float32))
    bout = np.asarray(L0_attn_b_out, dtype=np.float32)
    bfull = np.ascontiguousarray(np.broadcast_to(bout[None, :], (128, D)))
    wg = np.ascontiguousarray(np.asarray(L0_ffn_w_gate, dtype=np.float32))
    wu = np.ascontiguousarray(np.asarray(L0_ffn_w_up, dtype=np.float32))
    wd = np.ascontiguousarray(np.asarray(L0_ffn_w_down, dtype=np.float32))

    if "nc" not in _NC_CACHE:
        _NC_CACHE["nc"] = build_program()
    nc = _NC_CACHE["nc"]
    in_maps = [{"x": x[i], "wqkv": wqkv, "wout": wout, "bfull": bfull,
                "wg": wg, "wu": wu, "wd": wd} for i in range(8)]
    res = run_bass_kernel_spmd(nc, in_maps, core_ids=list(range(8)))
    return np.stack([np.asarray(r["out"]) for r in res.results], axis=0)
